# version 02 (final)
# baseline (speedup 1.0000x reference)
import contextlib
import numpy as np
import ml_dtypes

import concourse.bass as bass
import concourse.mybir as mybir
from concourse.bass_utils import run_bass_kernel_spmd

F32 = mybir.dt.float32
BF16 = mybir.dt.bfloat16
AF = mybir.ActivationFunctionType
ALU = mybir.AluOpType
AX = mybir.AxisListType
BFNP = ml_dtypes.bfloat16

NCORES = 8
TPC = 2048
HID = 4096
NTILE = TPC // 128
NCH = HID // 128
B = 4
L = 4096
NH = 32
NKV = 8
HD = 128
NCHK = 32
HPC = 16


def _rope0_tables():
    inv = 1.0 / (10000.0 ** (np.arange(0, HD, 2, dtype=np.float32) / HD))
    ang = np.arange(L, dtype=np.float32)[:, None] * inv[None, :]
    cos = np.concatenate([np.cos(ang), np.cos(ang)], axis=1)
    sin = np.concatenate([np.sin(ang), np.sin(ang)], axis=1)
    sgn = np.concatenate([-np.ones(64, np.float32), np.ones(64, np.float32)])
    return cos.astype(np.float32), (sin * sgn).astype(np.float32)


def _rope1_tables():
    HDv, THETA, FACTOR, HI_FF, LO_FF, ORIG_MAX = 128, 500000.0, 8.0, 4.0, 1.0, 8192
    inv = 1.0 / THETA ** (np.arange(0, HDv, 2, dtype=np.float64) / HDv)
    wavelen = 2.0 * np.pi / inv
    low_wl = ORIG_MAX / LO_FF
    high_wl = ORIG_MAX / HI_FF
    smooth = (ORIG_MAX / wavelen - LO_FF) / (HI_FF - LO_FF)
    scaled = np.where(wavelen > low_wl, inv / FACTOR, inv)
    mid = (wavelen <= low_wl) & (wavelen >= high_wl)
    scaled = np.where(mid, (1 - smooth) * inv / FACTOR + smooth * inv, scaled)
    inv = scaled.astype(np.float32)
    ang = np.arange(L, dtype=np.float32)[:, None] * inv[None, :]
    cos = np.concatenate([np.cos(ang), np.cos(ang)], axis=1)
    sin = np.concatenate([np.sin(ang), np.sin(ang)], axis=1)
    sgn = np.concatenate([-np.ones(64, np.float32), np.ones(64, np.float32)])
    return cos.astype(np.float32), (sin * sgn).astype(np.float32)


IDENT = np.eye(128, dtype=np.float32)
TRIU = np.triu(np.ones((128, 128), np.float32))


class Sems:
    def __init__(self, nc, es, names):
        self.h = {n: es.enter_context(nc.semaphore(n)) for n in names}
        self.c = {n: 0 for n in names}

    def inc(self, inst, n, k=1):
        if k > 0:
            inst.then_inc(self.h[n], k)
            self.c[n] += k

    def base(self):
        return dict(self.c)

    def wait(self, eng, n, v):
        if v > 0:
            eng.wait_ge(self.h[n], v)

    def link(self, inst, n):
        inst.then_inc(self.h[n], 1)
        self.c[n] += 1

    def after(self, eng, n):
        if self.c[n] > 0:
            eng.wait_ge(self.h[n], self.c[n])


def dram(nc, name, shape, dt, kind):
    return nc.dram_tensor(name, list(shape), dt, kind=kind)


def build_pre(layer):
    nc = bass.Bass("TRN2", target_bir_lowering=False)
    In, Out = "ExternalInput", "ExternalOutput"
    x_h = dram(nc, "x", [TPC, HID], F32, In)
    wq_h = dram(nc, "wq", [HID, HID], F32, In)
    wk_h = dram(nc, "wk", [HID, 1024], F32, In)
    wv_h = dram(nc, "wv", [HID, 1024], F32, In)
    rc_h = dram(nc, "rc", [TPC, 128], F32, In)
    rs_h = dram(nc, "rs", [TPC, 128], F32, In)
    rsr_h = dram(nc, "rsr", [TPC, 128], F32, In)
    idf_h = dram(nc, "idf", [128, 128], F32, In)
    if layer == 0:
        wg_h = dram(nc, "wg", [HID, HID], F32, In)
        win_h = dram(nc, "win", [HID, 32], F32, In)
        dtb_h = dram(nc, "dtb", [1, 32], F32, In)
        alog_h = dram(nc, "alog", [1, 32], F32, In)

    xnfm_h = dram(nc, "xnfm", [128, NCH * TPC], BF16, Out)
    qT_h = dram(nc, "qT", [NH, 128, TPC], BF16, Out)
    kT_h = dram(nc, "kT", [NKV, 128, TPC], BF16, Out)
    if layer == 0:
        ktm_h = dram(nc, "ktm", [TPC, 1024], BF16, Out)
        xc_h = dram(nc, "xc", [TPC, HID], BF16, Out)
        gate_h = dram(nc, "gate", [TPC, HID], BF16, Out)
        gC_h = dram(nc, "gC", [32, TPC], F32, Out)
    else:
        vtm_h = dram(nc, "vtm", [TPC, 1024], BF16, Out)

    es = contextlib.ExitStack()
    S = Sems(nc, es, ["ld", "st", "stg", "pl", "pe", "act", "dve", "ca", "cv", "cp"])

    def flush():
        with nc.Block() as blk:
            @blk.sync
            def _(sy):
                sy.wait_ge(S.h["ld"], S.c["ld"])
                if S.c["st"] > 0:
                    sy.wait_ge(S.h["st"], S.c["st"])

            @blk.gpsimd
            def _(g):
                if S.c["stg"] > 0:
                    g.wait_ge(S.h["stg"], S.c["stg"])
    xnfm = es.enter_context(nc.sbuf_tensor("xnfm_s", [128, NCH * TPC], BF16))
    idf = es.enter_context(nc.sbuf_tensor("idf_s", [128, 128], F32))
    idb = es.enter_context(nc.sbuf_tensor("idb", [128, 128], BF16))
    ones = es.enter_context(nc.sbuf_tensor("ones", [128, 8], F32))
    epsc = es.enter_context(nc.sbuf_tensor("epsc", [128, 1], F32))
    if layer == 0:
        winb = es.enter_context(nc.sbuf_tensor("winb", [128, NCH * 32], BF16))
        dtbb = es.enter_context(nc.sbuf_tensor("dtbb", [128, 32], F32))
        negA = es.enter_context(nc.sbuf_tensor("negA", [128, 32], F32))
        dt_all = es.enter_context(nc.sbuf_tensor("dt_all", [128, NTILE * 32], F32))

    with nc.Block() as blk:
        @blk.sync
        def _(sy):
            S.inc(sy.dma_start(idf[:], idf_h[:, :]), "ld", 16)
            sy.wait_ge(S.h["ld"], S.c["ld"])

        @blk.vector
        def _(v):
            S.wait(v, "ld", S.c["ld"])
            v.memset(ones[:], 1.0)
            v.memset(epsc[:], 1e-5)
            v.tensor_copy(idb[:], idf[:])

    if layer == 0:
        with contextlib.ExitStack() as es1:
            win_f = es1.enter_context(nc.sbuf_tensor("win_f", [128, NCH * 32], F32))
            with nc.Block() as blk:
                @blk.sync
                def _(sy):
                    S.inc(sy.dma_start(
                        win_f[:].rearrange("p (c o) -> p c o", c=NCH),
                        win_h[:, :].rearrange("(c p) o -> p c o", p=128)), "ld", 16)
                    sy.wait_ge(S.h["ld"], S.c["ld"])

                @blk.gpsimd
                def _(g):
                    S.wait(g, "ld", S.c["ld"])
                    g.tensor_copy(winb[:], win_f[:])

    if layer == 0:
        with contextlib.ExitStack() as es1:
            ones128 = es1.enter_context(nc.sbuf_tensor("ones128", [1, 128], F32))
            rowA2 = es1.enter_context(nc.sbuf_tensor("rowA2", [1, 32], F32))
            rowD2 = es1.enter_context(nc.sbuf_tensor("rowD2", [1, 32], F32))
            pbx = es1.enter_context(nc.psum_tensor("pbx", [128, 32], F32))
            with nc.Block() as blk:
                @blk.sync
                def _(sy):
                    S.inc(sy.dma_start(rowA2[:], alog_h[:, :]), "ld", 16)
                    S.inc(sy.dma_start(rowD2[:], dtb_h[:, :]), "ld", 16)
                    sy.wait_ge(S.h["ld"], S.c["ld"])

                @blk.vector
                def _(v):
                    S.wait(v, "ld", S.c["ld"])
                    v.memset(ones128[:], 1.0)

                @blk.scalar
                def _(sc):
                    S.wait(sc, "ld", S.c["ld"])
                    S.link(sc.activation(rowA2[:], rowA2[:], AF.Exp), "ca")
                    S.after(sc, "ca")
                    sc.mul(rowA2[:], rowA2[:], -1.0)
            with nc.Block() as blk:
                b0 = S.base()

                @blk.tensor
                def _(t):
                    S.inc(t.matmul(pbx[:], ones128[:], rowD2[:], start=True, stop=True), "pe")

                @blk.vector
                def _(v, b0=b0):
                    S.wait(v, "pe", b0["pe"] + 1)
                    v.tensor_copy(dtbb[:], pbx[:])
            with nc.Block() as blk:
                b0 = S.base()

                @blk.tensor
                def _(t):
                    S.inc(t.matmul(pbx[:], ones128[:], rowA2[:], start=True, stop=True), "pe")

                @blk.vector
                def _(v, b0=b0):
                    S.wait(v, "pe", b0["pe"] + 1)
                    v.tensor_copy(negA[:], pbx[:])

    with contextlib.ExitStack() as es2:
        xt = [es2.enter_context(nc.sbuf_tensor(f"xt{i}", [128, HID], F32)) for i in range(2)]
        xm = es2.enter_context(nc.sbuf_tensor("xm", [128, HID], F32))
        sq = es2.enter_context(nc.sbuf_tensor("sq", [128, HID], BF16))
        nm = es2.enter_context(nc.sbuf_tensor("nm", [128, 1], F32))
        ssq = es2.enter_context(nc.sbuf_tensor("ssq", [128, 1], F32))
        rstd = es2.enter_context(nc.sbuf_tensor("rstd", [128, 1], F32))
        xnbf = es2.enter_context(nc.sbuf_tensor("xnbf", [128, HID], BF16))
        ptr = [es2.enter_context(nc.psum_tensor(f"ptr{i}", [128, 128], BF16)) for i in range(2)]
        if layer == 0:
            pdt = es2.enter_context(nc.psum_tensor("pdt", [128, 32], F32))
            pgt = es2.enter_context(nc.psum_tensor("pgt", [32, 128], F32))
            dtz = es2.enter_context(nc.sbuf_tensor("dtz", [128, 32], F32))
            spA = es2.enter_context(nc.sbuf_tensor("spA", [128, 32], F32))
            spB = es2.enter_context(nc.sbuf_tensor("spB", [128, 32], F32))
            gti = es2.enter_context(nc.sbuf_tensor("gti", [128, 32], F32))
            gtr = es2.enter_context(nc.sbuf_tensor("gtr", [32, 128], F32))

        with nc.Block() as blk:
            @blk.sync
            def _(sy):
                S.inc(sy.dma_start(xt[0][:], x_h[0:128, :]), "ld", 16)
                sy.wait_ge(S.h["ld"], S.c["ld"])

        for i in range(NTILE):
            cur = xt[i % 2]
            with nc.Block() as blk:
                @blk.sync
                def _(sy, i=i):
                    if i + 1 < NTILE:
                        S.inc(sy.dma_start(xt[(i + 1) % 2][:],
                                           x_h[(i + 1) * 128:(i + 2) * 128, :]), "ld", 16)
                        sy.wait_ge(S.h["ld"], S.c["ld"])

                @blk.vector
                def _(v, cur=cur):
                    v.reduce_sum(nm[:], cur[:], axis=AX.X)
            with nc.Block() as blk:
                @blk.scalar
                def _(sc, cur=cur):
                    S.link(sc.mul(nm[:], nm[:], -1.0 / HID), "ca")
                    S.after(sc, "ca")
                    S.link(sc.activation(xm[:], cur[:], AF.Identity, bias=nm[:]), "ca")
                    S.after(sc, "ca")
                    sc.activation(sq[:], xm[:], AF.Square)
            with nc.Block() as blk:
                @blk.vector
                def _(v):
                    v.reduce_sum(ssq[:], sq[:], axis=AX.X)
            with nc.Block() as blk:
                @blk.scalar
                def _(sc):
                    sc.activation(ssq[:], ssq[:], AF.Sqrt, bias=epsc[:], scale=1.0 / HID)
            with nc.Block() as blk:
                @blk.vector
                def _(v):
                    S.link(v.reciprocal(rstd[:], ssq[:]), "cv")
                    S.after(v, "cv")
                    v.tensor_scalar_mul(xnbf[:], xm[:], rstd[:])
            with nc.Block() as blk:
                b0 = S.base()

                @blk.tensor
                def _(t, i=i, b0=b0):
                    for cc in range(NCH):
                        if cc >= 2:
                            S.wait(t, "dve", b0["dve"] + cc - 1)
                        S.inc(t.transpose(ptr[cc % 2][:], xnbf[:, cc * 128:(cc + 1) * 128], idb[:]), "pe")

                @blk.vector
                def _(v, i=i, b0=b0):
                    for cc in range(NCH):
                        S.wait(v, "pe", b0["pe"] + cc + 1)
                        S.inc(v.tensor_copy(xnfm[:, cc * TPC + i * 128: cc * TPC + (i + 1) * 128],
                                            ptr[cc % 2][:]), "dve")
            if layer == 0:
                with nc.Block() as blk:
                    b0 = S.base()

                    @blk.tensor
                    def _(t, i=i):
                        for cc in range(NCH):
                            S.inc(t.matmul(pdt[:], xnfm[:, cc * TPC + i * 128: cc * TPC + (i + 1) * 128],
                                           winb[:, cc * 32:(cc + 1) * 32],
                                           start=(cc == 0), stop=(cc == NCH - 1)), "pe", 0 if cc < NCH - 1 else 1)

                    @blk.vector
                    def _(v, b0=b0):
                        S.wait(v, "pe", b0["pe"] + 1)
                        v.tensor_add(dtz[:], dtbb[:], pdt[:])
                with nc.Block() as blk:
                    @blk.scalar
                    def _(sc):
                        S.link(sc.activation(spA[:], dtz[:], AF.Abs), "ca")
                        S.after(sc, "ca")
                        S.link(sc.activation(spA[:], spA[:], AF.Exp, scale=-1.0), "ca")
                        S.after(sc, "ca")
                        sc.activation(spA[:], spA[:], AF.Ln, bias=1.0)
                        sc.activation(spB[:], dtz[:], AF.Relu)
                with nc.Block() as blk:
                    @blk.vector
                    def _(v, i=i):
                        S.link(v.tensor_add(dt_all[:, i * 32:(i + 1) * 32], spA[:], spB[:]), "cv")
                        S.after(v, "cv")
                        v.tensor_mul(gti[:], dt_all[:, i * 32:(i + 1) * 32], negA[:])
                with nc.Block() as blk:
                    b0 = S.base()

                    @blk.tensor
                    def _(t):
                        S.inc(t.transpose(pgt[:], gti[:], idf[:]), "pe")

                    @blk.scalar
                    def _(sc, b0=b0):
                        S.wait(sc, "pe", b0["pe"] + 1)
                        S.wait(sc, "stg", b0["stg"])
                        S.inc(sc.copy(gtr[:], pgt[:]), "act")

                    @blk.gpsimd
                    def _(g, i=i, b0=b0):
                        S.wait(g, "act", b0["act"] + 1)
                        S.inc(g.dma_start(gC_h[:, i * 128:(i + 1) * 128], gtr[:]), "stg", 16)

    flush()

    rc_tm = es.enter_context(nc.sbuf_tensor("rc_tm", [128, NTILE * 128], F32))
    rs_tm = es.enter_context(nc.sbuf_tensor("rs_tm", [128, NTILE * 128], F32))
    with nc.Block() as blk:
        @blk.sync
        def _(sy):
            for i in range(NTILE):
                S.inc(sy.dma_start(rc_tm[:, i * 128:(i + 1) * 128],
                                   rc_h[i * 128:(i + 1) * 128, :]), "ld", 16)
                S.inc(sy.dma_start(rs_tm[:, i * 128:(i + 1) * 128],
                                   rs_h[i * 128:(i + 1) * 128, :]), "ld", 16)
            sy.wait_ge(S.h["ld"], S.c["ld"])

    rcf = es.enter_context(nc.sbuf_tensor("rcf", [128, TPC], F32))
    rsf = es.enter_context(nc.sbuf_tensor("rsf", [128, TPC], F32))
    with contextlib.ExitStack() as es3:
        prt = [es3.enter_context(nc.psum_tensor(f"prt{i}", [128, 128], F32)) for i in range(2)]
        rbuf = [es3.enter_context(nc.sbuf_tensor(f"rbuf{i}", [128, 128], F32)) for i in range(2)]
        with nc.Block() as blk:
            b0 = S.base()

            @blk.sync
            def _(sy, b0=b0):
                for i in range(NTILE):
                    if i >= 2:
                        S.wait(sy, "pe", b0["pe"] + 2 * (i - 1))
                    S.inc(sy.dma_start(rbuf[i % 2][:], rsr_h[i * 128:(i + 1) * 128, :]), "ld", 16)
                    sy.wait_ge(S.h["ld"], S.c["ld"])

            @blk.tensor
            def _(t, b0=b0):
                for i in range(NTILE):
                    if i >= 1:
                        S.wait(t, "act", b0["act"] + 2 * i - 1)
                    S.inc(t.transpose(prt[0][:], rc_tm[:, i * 128:(i + 1) * 128], idf[:]), "pe")
                    S.wait(t, "ld", b0["ld"] + 16 * (i + 1))
                    if i >= 1:
                        S.wait(t, "act", b0["act"] + 2 * i)
                    S.inc(t.transpose(prt[1][:], rbuf[i % 2][:], idf[:]), "pe")

            @blk.scalar
            def _(sc, b0=b0):
                for i in range(NTILE):
                    S.wait(sc, "pe", b0["pe"] + 2 * i + 1)
                    S.after(sc, "act")
                    S.inc(sc.copy(rcf[:, i * 128:(i + 1) * 128], prt[0][:]), "act")
                    S.wait(sc, "pe", b0["pe"] + 2 * i + 2)
                    S.after(sc, "act")
                    S.inc(sc.copy(rsf[:, i * 128:(i + 1) * 128], prt[1][:]), "act")

    def fm_jobs(w_h, n_oc, out_h):
        with contextlib.ExitStack() as es3:
            wf = [es3.enter_context(nc.sbuf_tensor(f"wf{w_h.name}{i}", [128, 128], F32)) for i in range(2)]
            wb = [es3.enter_context(nc.sbuf_tensor(f"wb{w_h.name}{i}", [128, 128], BF16)) for i in range(2)]
            pf = [es3.enter_context(nc.psum_tensor(f"pf{w_h.name}{i}", [128, 512], F32)) for i in range(4)]
            qsb = es3.enter_context(nc.sbuf_tensor(f"qs{w_h.name}", [128, TPC], F32))
            t1 = es3.enter_context(nc.sbuf_tensor(f"t1{w_h.name}", [128, TPC], F32))
            t2 = es3.enter_context(nc.sbuf_tensor(f"t2{w_h.name}", [128, TPC], F32))
            qbf = [es3.enter_context(nc.sbuf_tensor(f"qb{w_h.name}{i}", [128, TPC], BF16)) for i in range(2)]
            st_hist = []
            for oc in range(n_oc):
                with nc.Block() as blk:
                    b0 = S.base()

                    @blk.sync
                    def _(sy, oc=oc, b0=b0):
                        for cc in range(NCH):
                            if cc >= 2:
                                S.wait(sy, "pl", b0["pl"] + cc - 1)
                            S.inc(sy.dma_start(wf[cc % 2][:],
                                               w_h[cc * 128:(cc + 1) * 128, oc * 128:(oc + 1) * 128]), "ld", 16)

                    @blk.gpsimd
                    def _(g, oc=oc, b0=b0):
                        for cc in range(NCH):
                            S.wait(g, "ld", b0["ld"] + 32 * (cc // 2 + 1))
                            if cc >= 2:
                                S.wait(g, "pe", b0["pe"] + cc - 1)
                            S.inc(g.tensor_copy(wb[cc % 2][:], wf[cc % 2][:]), "pl")

                    @blk.tensor
                    def _(t, oc=oc, b0=b0):
                        for cc in range(NCH):
                            S.wait(t, "pl", b0["pl"] + cc + 1)
                            for tg in range(4):
                                S.inc(t.matmul(pf[tg][:], wb[cc % 2][:],
                                               xnfm[:, cc * TPC + tg * 512: cc * TPC + (tg + 1) * 512],
                                               start=(cc == 0), stop=(cc == NCH - 1)),
                                      "pe", 1 if tg == 3 else 0)

                    @blk.scalar
                    def _(sc, oc=oc, b0=b0):
                        S.wait(sc, "pe", b0["pe"] + NCH)
                        for tg in range(4):
                            S.inc(sc.copy(qsb[:, tg * 512:(tg + 1) * 512], pf[tg][:]), "act")

                    @blk.vector
                    def _(v, oc=oc, b0=b0):
                        S.wait(v, "act", b0["act"] + 4)
                        if len(st_hist) >= 1:
                            S.wait(v, "st", st_hist[-1])
                        dst = qbf[oc % 2]
                        v.tensor_mul(t1[0:64, :], qsb[64:128, :], rsf[64:128, :])
                        v.tensor_mul(t1[64:128, :], qsb[0:64, :], rsf[0:64, :])
                        S.link(v.tensor_mul(t2[:], qsb[:], rcf[:]), "cv")
                        S.after(v, "cv")
                        S.inc(v.tensor_add(dst[:], t2[:], t1[:]), "dve")

                with nc.Block() as blk:
                    b1 = S.base()

                    @blk.sync
                    def _(sy, oc=oc, b1=b1):
                        S.inc(sy.dma_start(out_h[oc, :, :], qbf[oc % 2][:]), "st", 16)
                st_hist.append(S.c["st"])

    fm_jobs(wq_h, NH, qT_h)
    flush()
    fm_jobs(wk_h, NKV, kT_h)
    flush()

    def tm_jobs(w_h, ncols, make_evac, extra_sb):
        n_oc = ncols // 512
        with contextlib.ExitStack() as es3:
            wf = [es3.enter_context(nc.sbuf_tensor(f"tf{w_h.name}{i}", [128, 512], F32)) for i in range(2)]
            wb = [es3.enter_context(nc.sbuf_tensor(f"tb{w_h.name}{i}", [128, 512], BF16)) for i in range(2)]
            pt = [es3.enter_context(nc.psum_tensor(f"pt{w_h.name}{i}", [128, 512], F32)) for i in range(8)]
            sbs = {}
            for nm_, shape, dt in extra_sb:
                sbs[nm_] = es3.enter_context(nc.sbuf_tensor(f"{nm_}_{w_h.name}", list(shape), dt))
            for tg in range(2):
                for oc in range(n_oc):
                    with nc.Block() as blk:
                        b0 = S.base()

                        @blk.sync
                        def _(sy, tg=tg, oc=oc, b0=b0):
                            for cc in range(NCH):
                                if cc >= 2:
                                    S.wait(sy, "pl", b0["pl"] + cc - 1)
                                S.inc(sy.dma_start(wf[cc % 2][:],
                                                   w_h[cc * 128:(cc + 1) * 128, oc * 512:(oc + 1) * 512]), "ld", 16)

                        @blk.gpsimd
                        def _(g, tg=tg, oc=oc, b0=b0):
                            for cc in range(NCH):
                                S.wait(g, "ld", b0["ld"] + 32 * (cc // 2 + 1))
                                if cc >= 2:
                                    S.wait(g, "pe", b0["pe"] + cc - 1)
                                S.inc(g.tensor_copy(wb[cc % 2][:], wf[cc % 2][:]), "pl")

                        @blk.tensor
                        def _(t, tg=tg, oc=oc, b0=b0):
                            for cc in range(NCH):
                                S.wait(t, "pl", b0["pl"] + cc + 1)
                                for tt in range(8):
                                    ti = tg * 8 + tt
                                    S.inc(t.matmul(pt[tt][:],
                                                   xnfm[:, cc * TPC + ti * 128: cc * TPC + (ti + 1) * 128],
                                                   wb[cc % 2][:],
                                                   start=(cc == 0), stop=(cc == NCH - 1)),
                                          "pe", 1 if tt == 7 else 0)
                        make_evac(blk, tg, oc, pt, sbs, b0)

    if layer == 0:
        def ev_gate(blk, tg, oc, pt, sbs, b0):
            @blk.scalar
            def _(sc):
                S.wait(sc, "pe", b0["pe"] + NCH)
                S.wait(sc, "stg", b0["stg"])
                for tt in range(8):
                    S.inc(sc.activation(sbs["gt"][:, tt * 512:(tt + 1) * 512], pt[tt][:], AF.Identity), "act")

            @blk.gpsimd
            def _(g):
                S.wait(g, "act", b0["act"] + 8)
                for tt in range(8):
                    ti = tg * 8 + tt
                    S.inc(g.dma_start(gate_h[ti * 128:(ti + 1) * 128, oc * 512:(oc + 1) * 512],
                                      sbs["gt"][:, tt * 512:(tt + 1) * 512]), "stg", 16)

        tm_jobs(wg_h, HID, ev_gate, [("gt", [128, 8 * 512], BF16)])
        flush()

        def ev_ktm(blk, tg, oc, pt, sbs, b0):
            @blk.scalar
            def _(sc):
                S.wait(sc, "pe", b0["pe"] + NCH)
                for tt in range(8):
                    S.inc(sc.copy(sbs["kf"][:, tt * 512:(tt + 1) * 512], pt[tt][:]), "act")

            cv_marks = []

            @blk.vector
            def _(v):
                S.wait(v, "act", b0["act"] + 8)
                S.wait(v, "stg", b0["stg"])
                for tt in range(8):
                    ti = tg * 8 + tt
                    ks = sbs["kf"][:, tt * 512:(tt + 1) * 512]
                    ko = sbs["ko"][:, tt * 512:(tt + 1) * 512]
                    tr = sbs["tr"]
                    rc_t = rc_tm[:, ti * 128:(ti + 1) * 128]
                    rs_t = rs_tm[:, ti * 128:(ti + 1) * 128]
                    for kh in range(4):
                        S.after(v, "cv")
                        v.tensor_mul(tr[:, 0:64], ks[:, kh * 128 + 64: kh * 128 + 128], rs_t[:, 0:64])
                        v.tensor_mul(tr[:, 64:128], ks[:, kh * 128: kh * 128 + 64], rs_t[:, 64:128])
                        S.link(v.tensor_mul(tr[:, 128:256], ks[:, kh * 128: kh * 128 + 128], rc_t[:]), "cv")
                        S.after(v, "cv")
                        S.link(v.tensor_add(ko[:, kh * 128:(kh + 1) * 128], tr[:, 128:256], tr[:, 0:128]), "cv")
                    cv_marks.append(S.c["cv"])

            @blk.gpsimd
            def _(g):
                for tt in range(8):
                    ti = tg * 8 + tt
                    g.wait_ge(S.h["cv"], cv_marks[tt])
                    S.inc(g.dma_start(ktm_h[ti * 128:(ti + 1) * 128, oc * 512:(oc + 1) * 512],
                                      sbs["ko"][:, tt * 512:(tt + 1) * 512]), "stg", 16)

        tm_jobs(wk_h, 1024, ev_ktm,
                [("kf", [128, 8 * 512], F32), ("ko", [128, 8 * 512], BF16), ("tr", [128, 256], F32)])
        flush()

        def ev_xc(blk, tg, oc, pt, sbs, b0):
            pe_end = b0["pe"] + NCH

            @blk.vector
            def _(v):
                for tt in range(8):
                    S.wait(v, "pe", pe_end)
                    S.wait(v, "stg", b0["stg"] + 16 * 16 * tt)
                    ti = tg * 8 + tt
                    for khl in range(4):
                        for rep in range(4):
                            h = (oc * 4 + khl) * 4 + rep
                            S.inc(v.tensor_scalar_mul(
                                sbs["xcs"][:, (khl * 4 + rep) * 128:(khl * 4 + rep + 1) * 128],
                                pt[tt][:, khl * 128:(khl + 1) * 128],
                                dt_all[:, ti * 32 + h: ti * 32 + h + 1]), "dve")

            @blk.gpsimd
            def _(g):
                for tt in range(8):
                    ti = tg * 8 + tt
                    S.wait(g, "dve", b0["dve"] + 16 * (tt + 1))
                    for khl in range(4):
                        for rep in range(4):
                            h = (oc * 4 + khl) * 4 + rep
                            S.inc(g.dma_start(xc_h[ti * 128:(ti + 1) * 128, h * 128:(h + 1) * 128],
                                              sbs["xcs"][:, (khl * 4 + rep) * 128:(khl * 4 + rep + 1) * 128]),
                                  "stg", 16)

        tm_jobs(wv_h, 1024, ev_xc, [("xcs", [128, 16 * 128], BF16)])
        flush()
    else:
        def ev_v(blk, tg, oc, pt, sbs, b0):
            @blk.scalar
            def _(sc):
                S.wait(sc, "pe", b0["pe"] + NCH)
                S.wait(sc, "stg", b0["stg"])
                for tt in range(8):
                    S.inc(sc.activation(sbs["vt"][:, tt * 512:(tt + 1) * 512], pt[tt][:], AF.Identity), "act")

            @blk.gpsimd
            def _(g):
                S.wait(g, "act", b0["act"] + 8)
                for tt in range(8):
                    ti = tg * 8 + tt
                    S.inc(g.dma_start(vtm_h[ti * 128:(ti + 1) * 128, oc * 512:(oc + 1) * 512],
                                      sbs["vt"][:, tt * 512:(tt + 1) * 512]), "stg", 16)

        tm_jobs(wv_h, 1024, ev_v, [("vt", [128, 8 * 512], BF16)])
        flush()

    with nc.Block() as blk:
        @blk.sync
        def _(sy):
            S.inc(sy.dma_start(xnfm_h[:, :], xnfm[:]), "st", 16)
            sy.wait_ge(S.h["st"], S.c["st"])
            sy.wait_ge(S.h["ld"], S.c["ld"])

        @blk.gpsimd
        def _(g):
            g.wait_ge(S.h["stg"], S.c["stg"])

    es.close()
    nc.finalize()
    return nc


def build_mamba():
    nc = bass.Bass("TRN2", target_bir_lowering=False)
    In, Out = "ExternalInput", "ExternalOutput"
    qTm_h = dram(nc, "qTm", [HPC, 128, L], BF16, In)
    kTm_h = dram(nc, "kTm", [4, 128, L], BF16, In)
    ktm2_h = dram(nc, "ktm2", [L, 512], BF16, In)
    xc2_h = dram(nc, "xc2", [L, HPC * 128], BF16, In)
    gC2_h = dram(nc, "gC2", [HPC, 32, 128], F32, In)
    triu_h = dram(nc, "triu", [128, 128], F32, In)
    maskn_h = dram(nc, "maskn", [128, 128], F32, In)
    idf_h = dram(nc, "idf", [128, 128], F32, In)
    y0p_h = dram(nc, "y0p", [L, HPC * 128], BF16, Out)

    es = contextlib.ExitStack()
    S = Sems(nc, es, ["ld", "st", "pe", "act", "dve"])
    triu_f = es.enter_context(nc.sbuf_tensor("triu_f", [128, 128], F32))
    triu_b = es.enter_context(nc.sbuf_tensor("triu_b", [128, 128], BF16))
    maskn = es.enter_context(nc.sbuf_tensor("maskn_s", [128, 128], F32))
    darg = es.enter_context(nc.sbuf_tensor("darg", [128, 128], F32))
    idf = es.enter_context(nc.sbuf_tensor("idf_s", [128, 128], F32))
    ones_f = es.enter_context(nc.sbuf_tensor("ones_f", [128, 128], F32))
    qh = [es.enter_context(nc.sbuf_tensor(f"qh{i}", [128, L], BF16)) for i in range(2)]
    kh = [es.enter_context(nc.sbuf_tensor(f"kh{i}", [128, L], BF16)) for i in range(2)]
    ktms = [es.enter_context(nc.sbuf_tensor(f"ktms{i}", [128, L], BF16)) for i in range(2)]
    xcs = [es.enter_context(nc.sbuf_tensor(f"xcs{i}", [128, L], BF16)) for i in range(2)]
    gcs = [es.enter_context(nc.sbuf_tensor(f"gcs{i}", [32, 128], F32)) for i in range(2)]
    g_pm = es.enter_context(nc.sbuf_tensor("g_pm", [128, 32], F32))
    G_sb = es.enter_context(nc.sbuf_tensor("G_sb", [128, 32], F32))
    negG = es.enter_context(nc.sbuf_tensor("negG", [128, 32], F32))
    tot_row = es.enter_context(nc.sbuf_tensor("tot_row", [1, 32], F32))
    tot_b = es.enter_context(nc.sbuf_tensor("tot_b", [128, 32], F32))
    kda = es.enter_context(nc.sbuf_tensor("kda", [128, 32], F32))
    expG = es.enter_context(nc.sbuf_tensor("expG", [128, 32], F32))
    kdecm = es.enter_context(nc.sbuf_tensor("kdecm", [128, 32], F32))
    cdexp = es.enter_context(nc.sbuf_tensor("cdexp", [128, 32], F32))
    grow = es.enter_context(nc.sbuf_tensor("grow", [1, 128], F32))
    decay_bf = es.enter_context(nc.sbuf_tensor("decay_bf", [128, 128], BF16))
    decay_m = es.enter_context(nc.sbuf_tensor("decay_m", [128, 128], BF16))
    P_sb = es.enter_context(nc.sbuf_tensor("P_sb", [128, 128], BF16))
    z_sb = es.enter_context(nc.sbuf_tensor("z_sb", [128, 128], F32))
    y_sb = [es.enter_context(nc.sbuf_tensor(f"y_sb{i}", [128, 128], BF16)) for i in range(2)]
    kdec_sb = es.enter_context(nc.sbuf_tensor("kdec_sb", [128, 128], BF16))
    state = [es.enter_context(nc.sbuf_tensor(f"state{i}", [128, 128], F32)) for i in range(2)]
    state_bf = es.enter_context(nc.sbuf_tensor("state_bf", [128, 128], BF16))
    ps_r = es.enter_context(nc.psum_tensor("ps_r", [1, 128], F32))
    ps_D = es.enter_context(nc.psum_tensor("ps_D", [128, 128], F32))
    ps_S = es.enter_context(nc.psum_tensor("ps_S", [128, 128], F32))
    ps_Z = es.enter_context(nc.psum_tensor("ps_Z", [128, 128], F32))
    ps_Y = es.enter_context(nc.psum_tensor("ps_Y", [128, 128], F32))
    ps_C = es.enter_context(nc.psum_tensor("ps_C", [128, 128], F32))
    ps_p = es.enter_context(nc.psum_tensor("ps_p", [128, 32], F32))
    ps_t = es.enter_context(nc.psum_tensor("ps_t", [1, 32], F32))

    def issue_loads(sy, j):
        S.inc(sy.dma_start(qh[j % 2][:], qTm_h[j, :, :]), "ld", 16)
        S.inc(sy.dma_start(kh[j % 2][:], kTm_h[j // 4, :, :]), "ld", 16)
        for n in range(32):
            S.inc(sy.dma_start(ktms[j % 2][:, n * 128:(n + 1) * 128],
                               ktm2_h[n * 128:(n + 1) * 128, (j // 4) * 128:(j // 4 + 1) * 128]), "ld", 16)
            S.inc(sy.dma_start(xcs[j % 2][:, n * 128:(n + 1) * 128],
                               xc2_h[n * 128:(n + 1) * 128, j * 128:(j + 1) * 128]), "ld", 16)
        S.inc(sy.dma_start(gcs[j % 2][:], gC2_h[j, :, :]), "ld", 16)

    with nc.Block() as blk:
        @blk.sync
        def _(sy):
            S.inc(sy.dma_start(triu_f[:], triu_h[:, :]), "ld", 16)
            S.inc(sy.dma_start(maskn[:], maskn_h[:, :]), "ld", 16)
            S.inc(sy.dma_start(idf[:], idf_h[:, :]), "ld", 16)
            issue_loads(sy, 0)
            sy.wait_ge(S.h["ld"], S.c["ld"])

        @blk.vector
        def _(v):
            S.wait(v, "ld", S.c["ld"])
            v.memset(ones_f[:], 1.0)
            v.tensor_copy(triu_b[:], triu_f[:])

    for hl in range(HPC):
        par = hl % 2
        with nc.Block() as blk:
            b0 = S.base()

            @blk.tensor
            def _(t, par=par):
                S.inc(t.transpose(ps_p[:], gcs[par][:], idf[0:32, 0:32]), "pe")

            @blk.scalar
            def _(sc, b0=b0):
                S.wait(sc, "pe", b0["pe"] + 1)
                sc.copy(g_pm[:], ps_p[:])
        with nc.Block() as blk:
            b0 = S.base()

            @blk.tensor
            def _(t):
                S.inc(t.matmul(ps_p[:], triu_f[:], g_pm[:], start=True, stop=True), "pe")
                S.inc(t.matmul(ps_t[:], ones_f[:, 0:1], g_pm[:], start=True, stop=True), "pe")

            @blk.scalar
            def _(sc, b0=b0):
                S.wait(sc, "pe", b0["pe"] + 2)
                S.link(sc.copy(G_sb[:], ps_p[:]), "act")
                S.after(sc, "act")
                sc.copy(tot_row[:], ps_t[:])
        with nc.Block() as blk:
            b0 = S.base()

            @blk.tensor
            def _(t):
                S.inc(t.matmul(ps_p[:], ones_f[0:1, :], tot_row[:], start=True, stop=True), "pe")

            @blk.scalar
            def _(sc, b0=b0):
                S.wait(sc, "pe", b0["pe"] + 1)
                sc.copy(tot_b[:], ps_p[:])
        with nc.Block() as blk:
            @blk.vector
            def _(v):
                S.link(v.tensor_scalar_mul(negG[:], G_sb[:], -1.0), "dve")
                S.after(v, "dve")
                S.link(v.tensor_sub(kda[:], tot_b[:], G_sb[:]), "dve")
                S.after(v, "dve")
                S.link(v.memset(state[0][:], 0.0), "dve")
                S.after(v, "dve")
                S.link(v.memset(state_bf[:], 0.0), "dve")
        with nc.Block() as blk:
            @blk.scalar
            def _(sc):
                S.link(sc.activation(expG[:], G_sb[:], AF.Exp), "act")
                S.after(sc, "act")
                S.link(sc.activation(kdecm[:], kda[:], AF.Exp), "act")
                S.after(sc, "act")
                S.link(sc.activation(cdexp[:], tot_b[:], AF.Exp), "act")

        with nc.Block() as blk:
            b0 = S.base()

            @blk.tensor
            def _(t, hl=hl, par=par, b0=b0):
                for n in range(32):
                    pe0 = b0["pe"] + 6 * n
                    a0 = b0["act"] + 2 * n
                    d0 = b0["dve"] + 7 * n
                    if n >= 1:
                        S.wait(t, "act", a0 - 1)
                    S.inc(t.transpose(ps_r[:], G_sb[:, n:n + 1], idf[:]), "pe")
                    S.wait(t, "act", a0 + 1)
                    if n >= 1:
                        S.wait(t, "dve", d0 - 6)
                    S.inc(t.matmul(ps_D[:], ones_f[0:1, :], grow[:], start=True, stop=True), "pe")
                    if n >= 1:
                        S.wait(t, "dve", d0 - 5)
                    S.inc(t.matmul(ps_S[:], kh[par][:, n * 128:(n + 1) * 128],
                                   qh[par][:, n * 128:(n + 1) * 128], start=True, stop=True), "pe")
                    S.wait(t, "dve", d0)
                    S.inc(t.matmul(ps_Z[:], qh[par][:, n * 128:(n + 1) * 128],
                                   state_bf[:], start=True, stop=True), "pe")
                    S.wait(t, "dve", d0 + 2)
                    S.inc(t.matmul(ps_Y[:], P_sb[:], xcs[par][:, n * 128:(n + 1) * 128],
                                   start=True, stop=True), "pe")
                    S.wait(t, "dve", d0 + 5)
                    S.inc(t.matmul(ps_C[:], kdec_sb[:], xcs[par][:, n * 128:(n + 1) * 128],
                                   start=True, stop=True), "pe")

            @blk.scalar
            def _(sc, b0=b0):
                for n in range(32):
                    pe0 = b0["pe"] + 6 * n
                    a0 = b0["act"] + 2 * n
                    d0 = b0["dve"] + 7 * n
                    S.wait(sc, "act", a0)
                    S.wait(sc, "pe", pe0 + 1)
                    S.inc(sc.copy(grow[:], ps_r[:]), "act")
                    S.wait(sc, "act", a0 + 1)
                    S.wait(sc, "dve", d0 + 1)
                    S.inc(sc.activation(decay_bf[:], darg[:], AF.Exp, bias=negG[:, n:n + 1]), "act")

            @blk.vector
            def _(v, hl=hl, par=par, b0=b0):
                for n in range(32):
                    pe0 = b0["pe"] + 6 * n
                    a0 = b0["act"] + 2 * n
                    d0 = b0["dve"] + 7 * n
                    S.wait(v, "dve", d0)
                    S.wait(v, "pe", pe0 + 2)
                    S.wait(v, "act", a0 + 2 - 2)
                    S.inc(v.tensor_add(darg[:], ps_D[:], maskn[:]), "dve")
                    S.wait(v, "dve", d0 + 1)
                    S.wait(v, "pe", pe0 + 3)
                    S.wait(v, "act", a0 + 2)
                    S.inc(v.tensor_mul(P_sb[:], ps_S[:], decay_bf[:]), "dve")
                    S.wait(v, "dve", d0 + 2)
                    S.wait(v, "pe", pe0 + 4)
                    S.inc(v.tensor_scalar_mul(z_sb[:], ps_Z[:], expG[:, n:n + 1]), "dve")
                    S.wait(v, "dve", d0 + 3)
                    S.wait(v, "pe", pe0 + 5)
                    S.wait(v, "st", b0["st"] + 32 * (n // 2) if n >= 2 else b0["st"])
                    S.inc(v.tensor_add(y_sb[n % 2][:], ps_Y[:], z_sb[:]), "dve")
                    S.wait(v, "dve", d0 + 4)
                    S.inc(v.tensor_scalar_mul(kdec_sb[:], ktms[par][:, n * 128:(n + 1) * 128],
                                              kdecm[:, n:n + 1]), "dve")
                    S.wait(v, "dve", d0 + 5)
                    S.wait(v, "pe", pe0 + 6)
                    S.inc(v.scalar_tensor_tensor(state[1 - n % 2][:], state[n % 2][:],
                                                 cdexp[:, n:n + 1], ps_C[:],
                                                 ALU.mult, ALU.add), "dve")
                    S.wait(v, "dve", d0 + 6)
                    S.inc(v.tensor_copy(state_bf[:], state[1 - n % 2][:]), "dve")

            @blk.sync
            def _(sy, hl=hl, b0=b0):
                for n in range(32):
                    d0 = b0["dve"] + 7 * n
                    sy.wait_ge(S.h["dve"], d0 + 4)
                    S.inc(sy.dma_start(y0p_h[n * 128:(n + 1) * 128, hl * 128:(hl + 1) * 128],
                                       y_sb[n % 2][:]), "st", 16)
                if hl + 1 < HPC:
                    issue_loads(sy, hl + 1)
                    sy.wait_ge(S.h["ld"], S.c["ld"])

    with nc.Block() as blk:
        @blk.sync
        def _(sy):
            sy.wait_ge(S.h["st"], S.c["st"])
            sy.wait_ge(S.h["ld"], S.c["ld"])

    es.close()
    nc.finalize()
    return nc


def build_post0():
    nc = bass.Bass("TRN2", target_bir_lowering=False)
    In, Out = "ExternalInput", "ExternalOutput"
    x_h = dram(nc, "x", [TPC, HID], F32, In)
    xnfm_h = dram(nc, "xnfm", [128, NCH * TPC], BF16, In)
    gate_h = dram(nc, "gate", [TPC, HID], BF16, In)
    y0_h = dram(nc, "y0", [TPC, HID], BF16, In)
    wo_h = dram(nc, "wo", [HID, HID], F32, In)
    wb_h = dram(nc, "wb", [HID, HID], F32, In)
    bb_h = dram(nc, "bb", [1, HID], F32, In)
    lr_h = dram(nc, "lr", [HID, 136], F32, In)
    bf_h = dram(nc, "bf", [128, HID], F32, In)
    gn_h = dram(nc, "gn", [1, 128], F32, In)
    idf_h = dram(nc, "idf", [128, 128], F32, In)
    x1_h = dram(nc, "x1", [TPC, HID], F32, Out)
    p1_d = dram(nc, "p1scr", [TPC, HID], F32, "Internal")

    es = contextlib.ExitStack()
    S = Sems(nc, es, ["ld", "lds", "st", "stg", "pl", "pe", "act", "dve", "l0", "l1", "l2", "l3"])

    def flush():
        with nc.Block() as blk:
            @blk.sync
            def _(sy):
                sy.wait_ge(S.h["ld"], S.c["ld"])
                if S.c["st"] > 0:
                    sy.wait_ge(S.h["st"], S.c["st"])

            @blk.gpsimd
            def _(g):
                if S.c["stg"] > 0:
                    g.wait_ge(S.h["stg"], S.c["stg"])

    idf = es.enter_context(nc.sbuf_tensor("idf_s", [128, 128], F32))
    idb = es.enter_context(nc.sbuf_tensor("idb_s", [128, 128], BF16))
    onesb = es.enter_context(nc.sbuf_tensor("onesb", [1, 128], BF16))
    epsc = es.enter_context(nc.sbuf_tensor("epsc", [128, 1], F32))
    gn_b = es.enter_context(nc.sbuf_tensor("gn_b", [128, 128], F32))

    ones_f = es.enter_context(nc.sbuf_tensor("ones_fr", [1, 128], F32))
    with contextlib.ExitStack() as es1:
        gn_r = es1.enter_context(nc.sbuf_tensor("gn_r", [1, 128], F32))
        pgn = es1.enter_context(nc.psum_tensor("pgn", [128, 128], F32))
        with nc.Block() as blk:
            @blk.sync
            def _(sy):
                S.inc(sy.dma_start(idf[:], idf_h[:, :]), "ld", 16)
                S.inc(sy.dma_start(gn_r[:], gn_h[:, :]), "ld", 16)
                sy.wait_ge(S.h["ld"], S.c["ld"])

            @blk.vector
            def _(v):
                S.wait(v, "ld", S.c["ld"])
                v.memset(onesb[:], 1.0)
                v.memset(ones_f[:], 1.0)
                v.memset(epsc[:], 1e-5)
                v.tensor_copy(idb[:], idf[:])
        with nc.Block() as blk:
            b0 = S.base()

            @blk.tensor
            def _(t):
                S.inc(t.matmul(pgn[:], ones_f[:], gn_r[:], start=True, stop=True), "pe")

            @blk.vector
            def _(v, b0=b0):
                S.wait(v, "pe", b0["pe"] + 1)
                v.tensor_copy(gn_b[:], pgn[:])

    with contextlib.ExitStack() as esx:
        lrb = esx.enter_context(nc.sbuf_tensor("lrb", [128, NCH * 136], BF16))
        bfb = esx.enter_context(nc.sbuf_tensor("bfb", [128, HID], BF16))
        bbb = esx.enter_context(nc.sbuf_tensor("bbb", [1, HID], BF16))
        hpfm = esx.enter_context(nc.sbuf_tensor("hpfm", [128, NTILE * 128], BF16))
        xnfm = esx.enter_context(nc.sbuf_tensor("xnfm_s", [128, NCH * TPC], BF16))
        with contextlib.ExitStack() as es1:
            lr_f = es1.enter_context(nc.sbuf_tensor("lr_f", [128, NCH * 136], F32))
            bf_f = es1.enter_context(nc.sbuf_tensor("bf_f", [128, HID], F32))
            bb_f = es1.enter_context(nc.sbuf_tensor("bb_f", [1, HID], F32))
            with nc.Block() as blk:
                @blk.sync
                def _(sy):
                    S.inc(sy.dma_start(lr_f[:].rearrange("p (c o) -> p c o", c=NCH),
                                       lr_h[:, :].rearrange("(c p) o -> p c o", p=128)), "ld", 16)
                    S.inc(sy.dma_start(bf_f[:], bf_h[:, :]), "ld", 16)
                    S.inc(sy.dma_start(bb_f[:], bb_h[:, :]), "ld", 16)
                    sy.wait_ge(S.h["ld"], S.c["ld"])

                @blk.gpsimd
                def _(g):
                    S.wait(g, "ld", S.c["ld"])
                    g.tensor_copy(lrb[:], lr_f[:])
                    g.tensor_copy(bfb[:], bf_f[:])

                @blk.vector
                def _(v):
                    S.wait(v, "ld", S.c["ld"])
                    v.tensor_copy(bbb[:], bb_f[:])
        with nc.Block() as blk:
            @blk.sync
            def _(sy):
                S.inc(sy.dma_start(xnfm[:], xnfm_h[:, :]), "ld", 16)
                sy.wait_ge(S.h["ld"], S.c["ld"])

        with contextlib.ExitStack() as es2:
            prl = es2.enter_context(nc.psum_tensor("prl", [128, 136], F32))
            ptp = es2.enter_context(nc.psum_tensor("ptp", [128, 128], BF16))
            lg = es2.enter_context(nc.sbuf_tensor("lg", [128, 8], F32))
            m1 = es2.enter_context(nc.sbuf_tensor("m1", [128, 1], F32))
            nm1 = es2.enter_context(nc.sbuf_tensor("nm1", [128, 1], F32))
            pex = es2.enter_context(nc.sbuf_tensor("pex", [128, 8], F32))
            ssum = es2.enter_context(nc.sbuf_tensor("ssum", [128, 1], F32))
            rs = es2.enter_context(nc.sbuf_tensor("rs_", [128, 1], F32))
            pp = es2.enter_context(nc.sbuf_tensor("pp", [128, 8], F32))
            m1p = es2.enter_context(nc.sbuf_tensor("m1p", [128, 1], F32))
            mask1 = es2.enter_context(nc.sbuf_tensor("mask1", [128, 8], F32))
            pwo = es2.enter_context(nc.sbuf_tensor("pwo", [128, 8], F32))
            m2 = es2.enter_context(nc.sbuf_tensor("m2", [128, 1], F32))
            den = es2.enter_context(nc.sbuf_tensor("den", [128, 1], F32))
            rd = es2.enter_context(nc.sbuf_tensor("rd", [128, 1], F32))
            keep = es2.enter_context(nc.sbuf_tensor("keep", [128, 8], F32))
            wun = es2.enter_context(nc.sbuf_tensor("wun", [128, 8], F32))
            w2 = es2.enter_context(nc.sbuf_tensor("w2", [128, 8], F32))
            hp = es2.enter_context(nc.sbuf_tensor("hp", [128, 128], BF16))
            for i in range(NTILE):
                with nc.Block() as blk:
                    b0 = S.base()

                    @blk.tensor
                    def _(t, i=i):
                        for cc in range(NCH):
                            S.inc(t.matmul(prl[:], xnfm[:, cc * TPC + i * 128: cc * TPC + (i + 1) * 128],
                                           lrb[:, cc * 136:(cc + 1) * 136],
                                           start=(cc == 0), stop=(cc == NCH - 1)),
                                  "pe", 1 if cc == NCH - 1 else 0)

                    @blk.scalar
                    def _(sc, b0=b0):
                        S.wait(sc, "pe", b0["pe"] + 1)
                        S.inc(sc.copy(lg[:], prl[:, 128:136]), "act")
                        S.wait(sc, "dve", b0["dve"] + 2)
                        S.inc(sc.activation(pex[:], lg[:], AF.Exp, bias=nm1[:]), "act")

                    @blk.vector
                    def _(v, b0=b0):
                        d = b0["dve"]
                        S.wait(v, "act", b0["act"] + 1)
                        S.inc(v.reduce_max(m1[:], lg[:], axis=AX.X), "dve")
                        S.wait(v, "dve", d + 1)
                        S.inc(v.tensor_scalar_mul(nm1[:], m1[:], -1.0), "dve")
                        S.wait(v, "act", b0["act"] + 2)
                        S.inc(v.reduce_sum(ssum[:], pex[:], axis=AX.X), "dve")
                        S.wait(v, "dve", d + 3)
                        S.inc(v.reciprocal(rs[:], ssum[:]), "dve")
                        S.wait(v, "dve", d + 4)
                        S.inc(v.tensor_scalar_mul(pp[:], pex[:], rs[:]), "dve")
                        S.wait(v, "dve", d + 5)
                        S.inc(v.reduce_max(m1p[:], pp[:], axis=AX.X), "dve")
                        S.wait(v, "dve", d + 6)
                        S.inc(v.tensor_scalar(mask1[:], pp[:], m1p[:], None, ALU.is_ge), "dve")
                        S.wait(v, "dve", d + 7)
                        S.inc(v.scalar_tensor_tensor(pwo[:], mask1[:], -1e9, pp[:], ALU.mult, ALU.add), "dve")
                        S.wait(v, "dve", d + 8)
                        S.inc(v.reduce_max(m2[:], pwo[:], axis=AX.X), "dve")
                        S.wait(v, "dve", d + 9)
                        S.inc(v.tensor_add(den[:], m1p[:], m2[:]), "dve")
                        S.wait(v, "dve", d + 10)
                        S.inc(v.reciprocal(rd[:], den[:]), "dve")
                        S.wait(v, "dve", d + 11)
                        S.inc(v.tensor_scalar(keep[:], pp[:], m2[:], None, ALU.is_ge), "dve")
                        S.wait(v, "dve", d + 12)
                        S.inc(v.tensor_mul(wun[:], pp[:], keep[:]), "dve")
                        S.wait(v, "dve", d + 13)
                        S.inc(v.tensor_scalar(w2[:], wun[:], rd[:], 2.0, ALU.mult, ALU.mult), "dve")
                        for e in range(8):
                            S.wait(v, "dve", d + 14 + e)
                            S.inc(v.tensor_scalar_mul(hp[:, e * 16:(e + 1) * 16],
                                                      prl[:, e * 16:(e + 1) * 16], w2[:, e:e + 1]), "dve")
                with nc.Block() as blk:
                    b0 = S.base()

                    @blk.tensor
                    def _(t):
                        S.inc(t.transpose(ptp[:], hp[:], idb[:]), "pe")

                    @blk.vector
                    def _(v, i=i, b0=b0):
                        S.wait(v, "pe", b0["pe"] + 1)
                        v.tensor_copy(hpfm[:, i * 128:(i + 1) * 128], ptp[:])

        with contextlib.ExitStack() as es3:
            wf = [es3.enter_context(nc.sbuf_tensor(f"bwf{i}", [128, 512], F32)) for i in range(2)]
            wbx = [es3.enter_context(nc.sbuf_tensor(f"bwb{i}", [128, 512], BF16)) for i in range(2)]
            pt = [es3.enter_context(nc.psum_tensor(f"bpt{i}", [128, 512], F32)) for i in range(8)]
            f1 = es3.enter_context(nc.sbuf_tensor("f1", [128, 8 * 512], F32))
            xq = es3.enter_context(nc.sbuf_tensor("xq", [128, 8 * 512], F32))
            for tg in range(2):
                for oc in range(8):
                    with nc.Block() as blk:
                        b0 = S.base()

                        @blk.sync
                        def _(sy, tg=tg, oc=oc, b0=b0):
                            for cc in range(NCH):
                                if cc >= 2:
                                    S.wait(sy, "pl", b0["pl"] + cc - 1)
                                S.inc(sy.dma_start(wf[cc % 2][:],
                                                   wb_h[cc * 128:(cc + 1) * 128, oc * 512:(oc + 1) * 512]), "ld", 16)
                            sy.wait_ge(S.h["ld"], S.c["ld"])

                        @blk.scalar
                        def _(sc, tg=tg, oc=oc, b0=b0):
                            S.wait(sc, "dve", b0["dve"])
                            for tt in range(8):
                                ti = tg * 8 + tt
                                S.inc(sc.dma_start(xq[:, tt * 512:(tt + 1) * 512],
                                                   x_h[ti * 128:(ti + 1) * 128, oc * 512:(oc + 1) * 512]), "lds", 16)
                            sc.wait_ge(S.h["lds"], S.c["lds"])

                        @blk.gpsimd
                        def _(g, tg=tg, oc=oc, b0=b0):
                            for cc in range(NCH):
                                S.wait(g, "ld", b0["ld"] + 32 * (cc // 2 + 1))
                                if cc >= 2:
                                    S.wait(g, "pe", b0["pe"] + cc - 1)
                                S.inc(g.tensor_copy(wbx[cc % 2][:], wf[cc % 2][:]), "pl")
                            S.wait(g, "dve", b0["dve"])
                            S.wait(g, "stg", b0["stg"])

                        @blk.tensor
                        def _(t, tg=tg, oc=oc, b0=b0):
                            for cc in range(NCH):
                                S.wait(t, "pl", b0["pl"] + cc + 1)
                                for tt in range(8):
                                    ti = tg * 8 + tt
                                    S.inc(t.matmul(pt[tt][:],
                                                   xnfm[:, cc * TPC + ti * 128: cc * TPC + (ti + 1) * 128],
                                                   wbx[cc % 2][:],
                                                   start=(cc == 0), stop=False),
                                          "pe", 1 if tt == 7 else 0)
                            for tt in range(8):
                                ti = tg * 8 + tt
                                t.matmul(pt[tt][:], onesb[:], bbb[0:1, oc * 512:(oc + 1) * 512],
                                         start=False, stop=False)
                                S.inc(t.matmul(pt[tt][:], hpfm[:, ti * 128:(ti + 1) * 128],
                                               bfb[:, oc * 512:(oc + 1) * 512], start=False, stop=True), "pe")

                        @blk.vector
                        def _(v, tg=tg, oc=oc, b0=b0):
                            S.wait(v, "pe", b0["pe"] + NCH + 8)
                            S.wait(v, "stg", b0["stg"])
                            S.wait(v, "lds", S.c["lds"])
                            for tt in range(8):
                                S.wait(v, "dve", S.c["dve"])
                                S.inc(v.tensor_add(f1[:, tt * 512:(tt + 1) * 512], pt[tt][:],
                                                   xq[:, tt * 512:(tt + 1) * 512]), "dve")
                    with nc.Block() as blk:
                        @blk.gpsimd
                        def _(g, tg=tg, oc=oc):
                            for tt in range(8):
                                ti = tg * 8 + tt
                                S.inc(g.dma_start(p1_d[ti * 128:(ti + 1) * 128, oc * 512:(oc + 1) * 512],
                                                  f1[:, tt * 512:(tt + 1) * 512]), "stg", 16)
        flush()

    with contextlib.ExitStack() as esy:
        ygfm = esy.enter_context(nc.sbuf_tensor("ygfm", [128, NCH * TPC], BF16))
        with contextlib.ExitStack() as es2:
            y0t = [es2.enter_context(nc.sbuf_tensor(f"y0t{i}", [128, HID], BF16)) for i in range(2)]
            gat = [es2.enter_context(nc.sbuf_tensor(f"gat{i}", [128, HID], BF16)) for i in range(2)]
            sqv = es2.enter_context(nc.sbuf_tensor("sqv", [128, HID], F32))
            msv = es2.enter_context(nc.sbuf_tensor("msv", [128, 32], F32))
            invr = es2.enter_context(nc.sbuf_tensor("invr", [128, 32], F32))
            sg = es2.enter_context(nc.sbuf_tensor("sg", [128, HID], F32))
            tmph = es2.enter_context(nc.sbuf_tensor("tmph", [128, 128], F32))
            ygt = es2.enter_context(nc.sbuf_tensor("ygt", [128, HID], BF16))
            ptr = [es2.enter_context(nc.psum_tensor(f"cptr{i}", [128, 128], BF16)) for i in range(2)]
            with nc.Block() as blk:
                @blk.sync
                def _(sy):
                    S.inc(sy.dma_start(y0t[0][:], y0_h[0:128, :]), "ld", 16)
                    S.inc(sy.dma_start(gat[0][:], gate_h[0:128, :]), "ld", 16)
                    sy.wait_ge(S.h["ld"], S.c["ld"])
            for i in range(NTILE):
                cy, cg = y0t[i % 2], gat[i % 2]
                with nc.Block() as blk:
                    b0 = S.base()

                    @blk.sync
                    def _(sy, i=i):
                        if i + 1 < NTILE:
                            S.inc(sy.dma_start(y0t[(i + 1) % 2][:],
                                               y0_h[(i + 1) * 128:(i + 2) * 128, :]), "ld", 16)
                            S.inc(sy.dma_start(gat[(i + 1) % 2][:],
                                               gate_h[(i + 1) * 128:(i + 2) * 128, :]), "ld", 16)
                            sy.wait_ge(S.h["ld"], S.c["ld"])

                    @blk.scalar
                    def _(sc, cy=cy, cg=cg, b0=b0):
                        S.link(sc.activation(sqv[:], cy[:], AF.Square), "act")
                        S.after(sc, "act")
                        S.link(sc.activation(sg[:], cg[:], AF.Silu), "act")

                    @blk.vector
                    def _(v, b0=b0):
                        S.wait(v, "act", b0["act"] + 1)
                        for h in range(32):
                            S.wait(v, "dve", S.c["dve"])
                            S.inc(v.reduce_sum(msv[:, h:h + 1], sqv[:, h * 128:(h + 1) * 128], axis=AX.X), "dve")
                with nc.Block() as blk:
                    @blk.scalar
                    def _(sc):
                        S.link(sc.activation(invr[:], msv[:], AF.Ln, bias=epsc[:], scale=1.0 / 128.0), "act")
                        S.after(sc, "act")
                        S.link(sc.activation(invr[:], invr[:], AF.Exp, scale=-0.5), "act")
                with nc.Block() as blk:
                    b0 = S.base()

                    @blk.vector
                    def _(v, cy=cy, b0=b0):
                        for h in range(32):
                            S.wait(v, "dve", S.c["dve"])
                            S.inc(v.scalar_tensor_tensor(tmph[:],
                                                         cy[:, h * 128:(h + 1) * 128],
                                                         invr[:, h:h + 1], gn_b[:],
                                                         ALU.mult, ALU.mult), "dve")
                            S.wait(v, "dve", S.c["dve"])
                            S.inc(v.tensor_mul(ygt[:, h * 128:(h + 1) * 128], tmph[:],
                                               sg[:, h * 128:(h + 1) * 128]), "dve")
                with nc.Block() as blk:
                    b0 = S.base()

                    @blk.tensor
                    def _(t, i=i, b0=b0):
                        for cc in range(NCH):
                            if cc >= 2:
                                S.wait(t, "dve", b0["dve"] + cc - 1)
                            S.inc(t.transpose(ptr[cc % 2][:], ygt[:, cc * 128:(cc + 1) * 128], idb[:]), "pe")

                    @blk.vector
                    def _(v, i=i, b0=b0):
                        for cc in range(NCH):
                            S.wait(v, "pe", b0["pe"] + cc + 1)
                            S.inc(v.tensor_copy(ygfm[:, cc * TPC + i * 128: cc * TPC + (i + 1) * 128],
                                                ptr[cc % 2][:]), "dve")
        with contextlib.ExitStack() as es3:
            wf = [es3.enter_context(nc.sbuf_tensor(f"dwf{i}", [128, 512], F32)) for i in range(2)]
            wbx = [es3.enter_context(nc.sbuf_tensor(f"dwb{i}", [128, 512], BF16)) for i in range(2)]
            pt = [es3.enter_context(nc.psum_tensor(f"dpt{i}", [128, 512], F32)) for i in range(8)]
            f1 = es3.enter_context(nc.sbuf_tensor("df1", [128, 8 * 512], F32))
            pq = es3.enter_context(nc.sbuf_tensor("dpq", [128, 8 * 512], F32))
            for tg in range(2):
                for oc in range(8):
                    with nc.Block() as blk:
                        b0 = S.base()

                        @blk.sync
                        def _(sy, tg=tg, oc=oc, b0=b0):
                            for cc in range(NCH):
                                if cc >= 2:
                                    S.wait(sy, "pl", b0["pl"] + cc - 1)
                                S.inc(sy.dma_start(wf[cc % 2][:],
                                                   wo_h[cc * 128:(cc + 1) * 128, oc * 512:(oc + 1) * 512]), "ld", 16)
                            sy.wait_ge(S.h["ld"], S.c["ld"])

                        @blk.scalar
                        def _(sc, tg=tg, oc=oc, b0=b0):
                            S.wait(sc, "dve", b0["dve"])
                            for tt in range(8):
                                ti = tg * 8 + tt
                                S.inc(sc.dma_start(pq[:, tt * 512:(tt + 1) * 512],
                                                   p1_d[ti * 128:(ti + 1) * 128, oc * 512:(oc + 1) * 512]), "lds", 16)
                            sc.wait_ge(S.h["lds"], S.c["lds"])

                        @blk.gpsimd
                        def _(g, tg=tg, oc=oc, b0=b0):
                            for cc in range(NCH):
                                S.wait(g, "ld", b0["ld"] + 32 * (cc // 2 + 1))
                                if cc >= 2:
                                    S.wait(g, "pe", b0["pe"] + cc - 1)
                                S.inc(g.tensor_copy(wbx[cc % 2][:], wf[cc % 2][:]), "pl")

                        @blk.tensor
                        def _(t, tg=tg, oc=oc, b0=b0):
                            for cc in range(NCH):
                                S.wait(t, "pl", b0["pl"] + cc + 1)
                                for tt in range(8):
                                    ti = tg * 8 + tt
                                    S.inc(t.matmul(pt[tt][:],
                                                   ygfm[:, cc * TPC + ti * 128: cc * TPC + (ti + 1) * 128],
                                                   wbx[cc % 2][:],
                                                   start=(cc == 0), stop=(cc == NCH - 1)),
                                          "pe", 1 if tt == 7 else 0)

                        @blk.vector
                        def _(v, tg=tg, oc=oc, b0=b0):
                            S.wait(v, "pe", b0["pe"] + NCH)
                            S.wait(v, "st", b0["st"])
                            S.wait(v, "lds", S.c["lds"])
                            for tt in range(8):
                                S.wait(v, "dve", S.c["dve"])
                                S.inc(v.tensor_add(f1[:, tt * 512:(tt + 1) * 512], pt[tt][:],
                                                   pq[:, tt * 512:(tt + 1) * 512]), "dve")
                    with nc.Block() as blk:
                        @blk.sync
                        def _(sy, tg=tg, oc=oc):
                            for tt in range(8):
                                ti = tg * 8 + tt
                                S.inc(sy.dma_start(x1_h[ti * 128:(ti + 1) * 128, oc * 512:(oc + 1) * 512],
                                                   f1[:, tt * 512:(tt + 1) * 512]), "st", 16)

    with nc.Block() as blk:
        @blk.sync
        def _(sy):
            sy.wait_ge(S.h["st"], S.c["st"])
            sy.wait_ge(S.h["ld"], S.c["ld"])

        @blk.gpsimd
        def _(g):
            g.wait_ge(S.h["stg"], S.c["stg"])

    es.close()
    nc.finalize()
    return nc


def build_post1():
    nc = bass.Bass("TRN2", target_bir_lowering=False)
    In, Out = "ExternalInput", "ExternalOutput"
    x1_h = dram(nc, "x1", [TPC, HID], F32, In)
    xnfm_h = dram(nc, "xnfm", [128, NCH * TPC], BF16, In)
    y1_h = dram(nc, "y1", [TPC, HID], BF16, In)
    w12_h = dram(nc, "w12", [HID, 2 * HID], F32, In)
    b12_h = dram(nc, "b12", [1, 2 * HID], F32, In)
    w3_h = dram(nc, "w3", [HID, HID], F32, In)
    b3_h = dram(nc, "b3", [1, HID], F32, In)
    wo_h = dram(nc, "wo", [HID, HID], F32, In)
    idf_h = dram(nc, "idf", [128, 128], F32, In)
    x2_h = dram(nc, "x2", [TPC, HID], F32, Out)
    x4_d = dram(nc, "x4scr", [TPC, HID], BF16, "Internal")
    p2_d = dram(nc, "p2scr", [TPC, HID], F32, "Internal")

    es = contextlib.ExitStack()
    S = Sems(nc, es, ["ld", "lds", "st", "stg", "pl", "pe", "act", "dve", "l0", "l1", "l2", "l3"])

    def flush():
        with nc.Block() as blk:
            @blk.sync
            def _(sy):
                sy.wait_ge(S.h["ld"], S.c["ld"])
                if S.c["st"] > 0:
                    sy.wait_ge(S.h["st"], S.c["st"])

            @blk.gpsimd
            def _(g):
                if S.c["stg"] > 0:
                    g.wait_ge(S.h["stg"], S.c["stg"])

    idf = es.enter_context(nc.sbuf_tensor("idf_s", [128, 128], F32))
    idb = es.enter_context(nc.sbuf_tensor("idb_s", [128, 128], BF16))
    onesb = es.enter_context(nc.sbuf_tensor("onesb", [1, 128], BF16))
    b12b = es.enter_context(nc.sbuf_tensor("b12b", [1, 2 * HID], BF16))
    b3b = es.enter_context(nc.sbuf_tensor("b3b", [1, HID], BF16))
    with contextlib.ExitStack() as es1:
        b12f = es1.enter_context(nc.sbuf_tensor("b12f", [1, 2 * HID], F32))
        b3f = es1.enter_context(nc.sbuf_tensor("b3f", [1, HID], F32))
        with nc.Block() as blk:
            @blk.sync
            def _(sy):
                S.inc(sy.dma_start(idf[:], idf_h[:, :]), "ld", 16)
                S.inc(sy.dma_start(b12f[:], b12_h[:, :]), "ld", 16)
                S.inc(sy.dma_start(b3f[:], b3_h[:, :]), "ld", 16)
                sy.wait_ge(S.h["ld"], S.c["ld"])

            @blk.vector
            def _(v):
                S.wait(v, "ld", S.c["ld"])
                v.memset(onesb[:], 1.0)
                v.tensor_copy(idb[:], idf[:])
                v.tensor_copy(b12b[:], b12f[:])
                v.tensor_copy(b3b[:], b3f[:])

    with contextlib.ExitStack() as esx:
        xnfm = esx.enter_context(nc.sbuf_tensor("xnfm_s", [128, NCH * TPC], BF16))
        with nc.Block() as blk:
            @blk.sync
            def _(sy):
                S.inc(sy.dma_start(xnfm[:], xnfm_h[:, :]), "ld", 16)
                sy.wait_ge(S.h["ld"], S.c["ld"])
        with contextlib.ExitStack() as es3:
            wfa = [es3.enter_context(nc.sbuf_tensor(f"awfa{i}", [128, 512], F32)) for i in range(2)]
            wfb = [es3.enter_context(nc.sbuf_tensor(f"awfb{i}", [128, 512], F32)) for i in range(2)]
            wba = [es3.enter_context(nc.sbuf_tensor(f"awba{i}", [128, 512], BF16)) for i in range(2)]
            wbb = [es3.enter_context(nc.sbuf_tensor(f"awbb{i}", [128, 512], BF16)) for i in range(2)]
            pa = [es3.enter_context(nc.psum_tensor(f"apa{i}", [128, 512], F32)) for i in range(4)]
            pb = [es3.enter_context(nc.psum_tensor(f"apb{i}", [128, 512], F32)) for i in range(4)]
            s1 = es3.enter_context(nc.sbuf_tensor("s1", [128, 4 * 512], F32))
            x4s = es3.enter_context(nc.sbuf_tensor("x4s", [128, 4 * 512], BF16))
            for tq in range(4):
                for oc in range(8):
                    with nc.Block() as blk:
                        b0 = S.base()

                        @blk.sync
                        def _(sy, tq=tq, oc=oc, b0=b0):
                            for j in range(2 * NCH):
                                if j >= 4:
                                    S.wait(sy, "pl", b0["pl"] + j - 3)
                                cc, hf = j // 2, j % 2
                                buf = wfa[cc % 2] if hf == 0 else wfb[cc % 2]
                                col = oc * 512 if hf == 0 else HID + oc * 512
                                S.inc(sy.dma_start(buf[:],
                                                   w12_h[cc * 128:(cc + 1) * 128, col: col + 512]),
                                      f"l{j % 4}", 16)
                            for r in range(4):
                                sy.wait_ge(S.h[f"l{r}"], S.c[f"l{r}"])

                        @blk.gpsimd
                        def _(g, tq=tq, oc=oc, b0=b0):
                            for j in range(2 * NCH):
                                cc, hf = j // 2, j % 2
                                S.wait(g, f"l{j % 4}", b0[f"l{j % 4}"] + 16 * (j // 4 + 1))
                                if cc >= 2:
                                    S.wait(g, "pe", b0["pe"] + cc - 1)
                                src_ = wfa[cc % 2] if hf == 0 else wfb[cc % 2]
                                dst_ = wba[cc % 2] if hf == 0 else wbb[cc % 2]
                                S.inc(g.tensor_copy(dst_[:], src_[:]), "pl")

                        @blk.tensor
                        def _(t, tq=tq, oc=oc, b0=b0):
                            for cc in range(NCH):
                                S.wait(t, "pl", b0["pl"] + 2 * (cc + 1))
                                for tt in range(4):
                                    ti = tq * 4 + tt
                                    t.matmul(pa[tt][:],
                                             xnfm[:, cc * TPC + ti * 128: cc * TPC + (ti + 1) * 128],
                                             wba[cc % 2][:], start=(cc == 0), stop=False)
                                    S.inc(t.matmul(pb[tt][:],
                                                   xnfm[:, cc * TPC + ti * 128: cc * TPC + (ti + 1) * 128],
                                                   wbb[cc % 2][:], start=(cc == 0), stop=False),
                                          "pe", 1 if tt == 3 else 0)
                            for tt in range(4):
                                t.matmul(pa[tt][:], onesb[:], b12b[0:1, oc * 512:(oc + 1) * 512],
                                         start=False, stop=True)
                                S.inc(t.matmul(pb[tt][:], onesb[:], b12b[0:1, HID + oc * 512: HID + (oc + 1) * 512],
                                               start=False, stop=True), "pe")

                        @blk.scalar
                        def _(sc, b0=b0):
                            S.wait(sc, "pe", b0["pe"] + NCH + 4)
                            for tt in range(4):
                                S.after(sc, "act")
                                S.inc(sc.activation(s1[:, tt * 512:(tt + 1) * 512], pa[tt][:], AF.Silu), "act")

                        @blk.vector
                        def _(v, b0=b0):
                            S.wait(v, "act", b0["act"] + 4)
                            S.wait(v, "stg", b0["stg"])
                            for tt in range(4):
                                S.wait(v, "dve", S.c["dve"])
                                S.inc(v.tensor_mul(x4s[:, tt * 512:(tt + 1) * 512],
                                                   s1[:, tt * 512:(tt + 1) * 512], pb[tt][:]), "dve")
                    with nc.Block() as blk:
                        @blk.gpsimd
                        def _(g, tq=tq, oc=oc):
                            for tt in range(4):
                                ti = tq * 4 + tt
                                S.inc(g.dma_start(x4_d[ti * 128:(ti + 1) * 128, oc * 512:(oc + 1) * 512],
                                                  x4s[:, tt * 512:(tt + 1) * 512]), "stg", 16)
        flush()

    with contextlib.ExitStack() as esy:
        x4fm = esy.enter_context(nc.sbuf_tensor("x4fm", [128, NCH * TPC], BF16))
        with contextlib.ExitStack() as es2:
            x4t = [es2.enter_context(nc.sbuf_tensor(f"x4t{i}", [128, HID], BF16)) for i in range(2)]
            ptr = [es2.enter_context(nc.psum_tensor(f"bptr{i}", [128, 128], BF16)) for i in range(2)]
            with nc.Block() as blk:
                @blk.sync
                def _(sy):
                    S.inc(sy.dma_start(x4t[0][:], x4_d[0:128, :]), "ld", 16)
                    sy.wait_ge(S.h["ld"], S.c["ld"])
            for i in range(NTILE):
                cx = x4t[i % 2]
                with nc.Block() as blk:
                    b0 = S.base()

                    @blk.sync
                    def _(sy, i=i):
                        if i + 1 < NTILE:
                            S.inc(sy.dma_start(x4t[(i + 1) % 2][:],
                                               x4_d[(i + 1) * 128:(i + 2) * 128, :]), "ld", 16)
                            sy.wait_ge(S.h["ld"], S.c["ld"])

                    @blk.tensor
                    def _(t, i=i, cx=cx, b0=b0):
                        for cc in range(NCH):
                            if cc >= 2:
                                S.wait(t, "dve", b0["dve"] + cc - 1)
                            S.inc(t.transpose(ptr[cc % 2][:], cx[:, cc * 128:(cc + 1) * 128], idb[:]), "pe")

                    @blk.vector
                    def _(v, i=i, b0=b0):
                        for cc in range(NCH):
                            S.wait(v, "pe", b0["pe"] + cc + 1)
                            S.inc(v.tensor_copy(x4fm[:, cc * TPC + i * 128: cc * TPC + (i + 1) * 128],
                                                ptr[cc % 2][:]), "dve")
        with contextlib.ExitStack() as es3:
            wf = [es3.enter_context(nc.sbuf_tensor(f"cwf{i}", [128, 512], F32)) for i in range(2)]
            wbx = [es3.enter_context(nc.sbuf_tensor(f"cwb{i}", [128, 512], BF16)) for i in range(2)]
            pt = [es3.enter_context(nc.psum_tensor(f"cpt{i}", [128, 512], F32)) for i in range(8)]
            f1 = es3.enter_context(nc.sbuf_tensor("cf1", [128, 8 * 512], F32))
            xq = es3.enter_context(nc.sbuf_tensor("cxq", [128, 8 * 512], F32))
            for tg in range(2):
                for oc in range(8):
                    with nc.Block() as blk:
                        b0 = S.base()

                        @blk.sync
                        def _(sy, tg=tg, oc=oc, b0=b0):
                            for cc in range(NCH):
                                if cc >= 2:
                                    S.wait(sy, "pl", b0["pl"] + cc - 1)
                                S.inc(sy.dma_start(wf[cc % 2][:],
                                                   w3_h[cc * 128:(cc + 1) * 128, oc * 512:(oc + 1) * 512]), "ld", 16)
                            sy.wait_ge(S.h["ld"], S.c["ld"])

                        @blk.scalar
                        def _(sc, tg=tg, oc=oc, b0=b0):
                            S.wait(sc, "dve", b0["dve"])
                            for tt in range(8):
                                ti = tg * 8 + tt
                                S.inc(sc.dma_start(xq[:, tt * 512:(tt + 1) * 512],
                                                   x1_h[ti * 128:(ti + 1) * 128, oc * 512:(oc + 1) * 512]), "lds", 16)
                            sc.wait_ge(S.h["lds"], S.c["lds"])

                        @blk.gpsimd
                        def _(g, tg=tg, oc=oc, b0=b0):
                            for cc in range(NCH):
                                S.wait(g, "ld", b0["ld"] + 32 * (cc // 2 + 1))
                                if cc >= 2:
                                    S.wait(g, "pe", b0["pe"] + cc - 1)
                                S.inc(g.tensor_copy(wbx[cc % 2][:], wf[cc % 2][:]), "pl")

                        @blk.tensor
                        def _(t, tg=tg, oc=oc, b0=b0):
                            for cc in range(NCH):
                                S.wait(t, "pl", b0["pl"] + cc + 1)
                                for tt in range(8):
                                    ti = tg * 8 + tt
                                    S.inc(t.matmul(pt[tt][:],
                                                   x4fm[:, cc * TPC + ti * 128: cc * TPC + (ti + 1) * 128],
                                                   wbx[cc % 2][:],
                                                   start=(cc == 0), stop=False),
                                          "pe", 1 if tt == 7 else 0)
                            for tt in range(8):
                                S.inc(t.matmul(pt[tt][:], onesb[:], b3b[0:1, oc * 512:(oc + 1) * 512],
                                               start=False, stop=True), "pe", 1 if tt == 7 else 0)

                        @blk.vector
                        def _(v, tg=tg, oc=oc, b0=b0):
                            S.wait(v, "pe", b0["pe"] + NCH + 1)
                            S.wait(v, "stg", b0["stg"])
                            S.wait(v, "lds", S.c["lds"])
                            for tt in range(8):
                                S.wait(v, "dve", S.c["dve"])
                                S.inc(v.tensor_add(f1[:, tt * 512:(tt + 1) * 512], pt[tt][:],
                                                   xq[:, tt * 512:(tt + 1) * 512]), "dve")
                    with nc.Block() as blk:
                        @blk.gpsimd
                        def _(g, tg=tg, oc=oc):
                            for tt in range(8):
                                ti = tg * 8 + tt
                                S.inc(g.dma_start(p2_d[ti * 128:(ti + 1) * 128, oc * 512:(oc + 1) * 512],
                                                  f1[:, tt * 512:(tt + 1) * 512]), "stg", 16)
        flush()

    with contextlib.ExitStack() as esy:
        y1fm = esy.enter_context(nc.sbuf_tensor("y1fm", [128, NCH * TPC], BF16))
        with contextlib.ExitStack() as es2:
            y1t = [es2.enter_context(nc.sbuf_tensor(f"y1t{i}", [128, HID], BF16)) for i in range(2)]
            ptr = [es2.enter_context(nc.psum_tensor(f"eptr{i}", [128, 128], BF16)) for i in range(2)]
            with nc.Block() as blk:
                @blk.sync
                def _(sy):
                    S.inc(sy.dma_start(y1t[0][:], y1_h[0:128, :]), "ld", 16)
                    sy.wait_ge(S.h["ld"], S.c["ld"])
            for i in range(NTILE):
                cx = y1t[i % 2]
                with nc.Block() as blk:
                    b0 = S.base()

                    @blk.sync
                    def _(sy, i=i):
                        if i + 1 < NTILE:
                            S.inc(sy.dma_start(y1t[(i + 1) % 2][:],
                                               y1_h[(i + 1) * 128:(i + 2) * 128, :]), "ld", 16)
                            sy.wait_ge(S.h["ld"], S.c["ld"])

                    @blk.tensor
                    def _(t, i=i, cx=cx, b0=b0):
                        for cc in range(NCH):
                            if cc >= 2:
                                S.wait(t, "dve", b0["dve"] + cc - 1)
                            S.inc(t.transpose(ptr[cc % 2][:], cx[:, cc * 128:(cc + 1) * 128], idb[:]), "pe")

                    @blk.vector
                    def _(v, i=i, b0=b0):
                        for cc in range(NCH):
                            S.wait(v, "pe", b0["pe"] + cc + 1)
                            S.inc(v.tensor_copy(y1fm[:, cc * TPC + i * 128: cc * TPC + (i + 1) * 128],
                                                ptr[cc % 2][:]), "dve")
        with contextlib.ExitStack() as es3:
            wf = [es3.enter_context(nc.sbuf_tensor(f"ewf{i}", [128, 512], F32)) for i in range(2)]
            wbx = [es3.enter_context(nc.sbuf_tensor(f"ewb{i}", [128, 512], BF16)) for i in range(2)]
            pt = [es3.enter_context(nc.psum_tensor(f"ept{i}", [128, 512], F32)) for i in range(8)]
            f1 = es3.enter_context(nc.sbuf_tensor("ef1", [128, 8 * 512], F32))
            xq = es3.enter_context(nc.sbuf_tensor("exq", [128, 8 * 512], F32))
            for tg in range(2):
                for oc in range(8):
                    with nc.Block() as blk:
                        b0 = S.base()

                        @blk.sync
                        def _(sy, tg=tg, oc=oc, b0=b0):
                            for cc in range(NCH):
                                if cc >= 2:
                                    S.wait(sy, "pl", b0["pl"] + cc - 1)
                                S.inc(sy.dma_start(wf[cc % 2][:],
                                                   wo_h[cc * 128:(cc + 1) * 128, oc * 512:(oc + 1) * 512]), "ld", 16)
                            sy.wait_ge(S.h["ld"], S.c["ld"])

                        @blk.scalar
                        def _(sc, tg=tg, oc=oc, b0=b0):
                            S.wait(sc, "dve", b0["dve"])
                            for tt in range(8):
                                ti = tg * 8 + tt
                                S.inc(sc.dma_start(xq[:, tt * 512:(tt + 1) * 512],
                                                   p2_d[ti * 128:(ti + 1) * 128, oc * 512:(oc + 1) * 512]), "lds", 16)
                            sc.wait_ge(S.h["lds"], S.c["lds"])

                        @blk.gpsimd
                        def _(g, tg=tg, oc=oc, b0=b0):
                            for cc in range(NCH):
                                S.wait(g, "ld", b0["ld"] + 32 * (cc // 2 + 1))
                                if cc >= 2:
                                    S.wait(g, "pe", b0["pe"] + cc - 1)
                                S.inc(g.tensor_copy(wbx[cc % 2][:], wf[cc % 2][:]), "pl")

                        @blk.tensor
                        def _(t, tg=tg, oc=oc, b0=b0):
                            for cc in range(NCH):
                                S.wait(t, "pl", b0["pl"] + cc + 1)
                                for tt in range(8):
                                    ti = tg * 8 + tt
                                    S.inc(t.matmul(pt[tt][:],
                                                   y1fm[:, cc * TPC + ti * 128: cc * TPC + (ti + 1) * 128],
                                                   wbx[cc % 2][:],
                                                   start=(cc == 0), stop=(cc == NCH - 1)),
                                          "pe", 1 if tt == 7 else 0)

                        @blk.vector
                        def _(v, tg=tg, oc=oc, b0=b0):
                            S.wait(v, "pe", b0["pe"] + NCH)
                            S.wait(v, "st", b0["st"])
                            S.wait(v, "lds", S.c["lds"])
                            for tt in range(8):
                                S.wait(v, "dve", S.c["dve"])
                                S.inc(v.tensor_add(f1[:, tt * 512:(tt + 1) * 512], pt[tt][:],
                                                   xq[:, tt * 512:(tt + 1) * 512]), "dve")
                    with nc.Block() as blk:
                        @blk.sync
                        def _(sy, tg=tg, oc=oc):
                            for tt in range(8):
                                ti = tg * 8 + tt
                                S.inc(sy.dma_start(x2_h[ti * 128:(ti + 1) * 128, oc * 512:(oc + 1) * 512],
                                                   f1[:, tt * 512:(tt + 1) * 512]), "st", 16)

    with nc.Block() as blk:
        @blk.sync
        def _(sy):
            sy.wait_ge(S.h["st"], S.c["st"])
            sy.wait_ge(S.h["ld"], S.c["ld"])

        @blk.gpsimd
        def _(g):
            g.wait_ge(S.h["stg"], S.c["stg"])

    es.close()
    nc.finalize()
    return nc


def build_attn():
    nc = bass.Bass("TRN2", target_bir_lowering=False)
    In, Out = "ExternalInput", "ExternalOutput"
    q1_h = dram(nc, "q1", [HPC, 128, L], BF16, In)
    k1_h = dram(nc, "k1", [4, 128, L], BF16, In)
    v1_h = dram(nc, "v1", [L, 512], BF16, In)
    triu_h = dram(nc, "triu", [128, 128], F32, In)
    y1p_h = dram(nc, "y1p", [L, HPC * 128], BF16, Out)

    es = contextlib.ExitStack()
    S = Sems(nc, es, ["ld", "st", "pe", "act", "dve"])
    triu_f = es.enter_context(nc.sbuf_tensor("triu_f", [128, 128], F32))
    triu_b = es.enter_context(nc.sbuf_tensor("triu_b", [128, 128], BF16))
    ones_c = es.enter_context(nc.sbuf_tensor("ones_c", [128, 1], F32))
    ones_r = es.enter_context(nc.sbuf_tensor("ones_r", [1, 128], F32))
    kbuf = es.enter_context(nc.sbuf_tensor("kbuf", [128, L], BF16))
    vext = es.enter_context(nc.sbuf_tensor("vext", [128, 32 * 132], BF16))
    qbuf = [es.enter_context(nc.sbuf_tensor(f"qbuf{i}", [128, L], BF16)) for i in range(2)]
    sqb = es.enter_context(nc.sbuf_tensor("sqb", [128, 512], F32))
    n2row = es.enter_context(nc.sbuf_tensor("n2row", [1, L], F32))
    mk2 = es.enter_context(nc.sbuf_tensor("mk2", [1, 1], F32))
    mq2 = es.enter_context(nc.sbuf_tensor("mq2", [1, 1], F32))
    cc1 = es.enter_context(nc.sbuf_tensor("cc1", [1, 1], F32))
    negc = es.enter_context(nc.sbuf_tensor("negc", [128, 1], F32))
    P_sb = es.enter_context(nc.sbuf_tensor("P_sb", [128, 32 * 512], BF16))
    rcol = es.enter_context(nc.sbuf_tensor("rcol", [128, 1], F32))
    yout = [es.enter_context(nc.sbuf_tensor(f"yout{i}", [128, 128], BF16)) for i in range(2)]
    ps_s = [es.enter_context(nc.psum_tensor(f"ps_s{i}", [128, 512], F32)) for i in range(2)]
    ps_o = [es.enter_context(nc.psum_tensor(f"ps_o{i}", [128, 132], F32)) for i in range(2)]
    ps_n = es.enter_context(nc.psum_tensor("ps_n", [1, 512], F32))
    ps_b = es.enter_context(nc.psum_tensor("ps_b", [128, 1], F32))

    with nc.Block() as blk:
        @blk.sync
        def _(sy):
            S.inc(sy.dma_start(triu_f[:], triu_h[:, :]), "ld", 16)
            S.inc(sy.dma_start(qbuf[0][:], q1_h[0, :, :]), "ld", 16)
            sy.wait_ge(S.h["ld"], S.c["ld"])

        @blk.vector
        def _(v):
            S.wait(v, "ld", S.c["ld"])
            v.memset(ones_c[:], 1.0)
            v.memset(ones_r[:], 1.0)
            v.tensor_copy(triu_b[:], triu_f[:])

    def norms_into(blk, src_bf, out_scalar, b0):
        @blk.scalar
        def _(sc):
            for c in range(8):
                S.after(sc, "act")
                if c >= 1:
                    S.wait(sc, "pe", b0["pe"] + c)
                S.inc(sc.activation(sqb[:], src_bf[:, c * 512:(c + 1) * 512], AF.Square), "act")

        @blk.tensor
        def _(t):
            for c in range(8):
                S.wait(t, "act", b0["act"] + c + 1)
                if c >= 1:
                    S.wait(t, "dve", b0["dve"] + c)
                S.inc(t.matmul(ps_n[:], ones_c[:], sqb[:], start=True, stop=True), "pe")

        @blk.vector
        def _(v):
            for c in range(8):
                S.wait(v, "pe", b0["pe"] + c + 1)
                S.after(v, "dve")
                S.inc(v.tensor_copy(n2row[0:1, c * 512:(c + 1) * 512], ps_n[:]), "dve")
            S.after(v, "dve")
            S.inc(v.reduce_max(out_scalar[:], n2row[:], axis=AX.X), "dve")

    for kg in range(4):
        with nc.Block() as blk:
            b0 = S.base()

            @blk.sync
            def _(sy, kg=kg):
                S.inc(sy.dma_start(kbuf[:], k1_h[kg, :, :]), "ld", 16)
                for j in range(32):
                    S.inc(sy.dma_start(vext[:, j * 132: j * 132 + 128],
                                       v1_h[j * 128:(j + 1) * 128, kg * 128:(kg + 1) * 128]), "ld", 16)
                sy.wait_ge(S.h["ld"], S.c["ld"])

            @blk.vector
            def _(v, b0=b0):
                S.wait(v, "ld", S.c["ld"])
                for j in range(32):
                    S.after(v, "dve")
                    S.inc(v.memset(vext[:, j * 132 + 128: (j + 1) * 132], 0.0), "dve")
                    S.after(v, "dve")
                    S.inc(v.memset(vext[:, j * 132 + 128: j * 132 + 129], 1.0), "dve")
        with nc.Block() as blk:
            b0 = S.base()
            norms_into(blk, kbuf, mk2, b0)
        for qh in range(4):
            h = kg * 4 + qh
            qb = qbuf[h % 2]
            with nc.Block() as blk:
                b0 = S.base()

                @blk.sync
                def _(sy, h=h):
                    if h + 1 < HPC:
                        S.inc(sy.dma_start(qbuf[(h + 1) % 2][:], q1_h[h + 1, :, :]), "ld", 16)
                        sy.wait_ge(S.h["ld"], S.c["ld"])
                norms_into(blk, qb, mq2, b0)
            with nc.Block() as blk:
                b0 = S.base()

                @blk.vector
                def _(v):
                    S.inc(v.tensor_mul(cc1[:], mq2[:], mk2[:]), "dve")

                @blk.scalar
                def _(sc, b0=b0):
                    S.wait(sc, "dve", b0["dve"] + 1)
                    S.link(sc.activation(cc1[:], cc1[:], AF.Sqrt), "act")
                    S.after(sc, "act")
                    S.inc(sc.mul(cc1[:], cc1[:], -0.0883883476483184), "act")
            with nc.Block() as blk:
                b0 = S.base()

                @blk.tensor
                def _(t):
                    S.inc(t.matmul(ps_b[:], ones_r[:], cc1[:], start=True, stop=True), "pe")

                @blk.scalar
                def _(sc, b0=b0):
                    S.wait(sc, "pe", b0["pe"] + 1)
                    sc.copy(negc[:], ps_b[:])
            for tc in range(8):
                nst = 4 * tc + 4
                with nc.Block() as blk:
                    b0 = S.base()

                    @blk.tensor
                    def _(t, tc=tc, qb=qb, b0=b0, nst=nst):
                        for st in range(nst):
                            if st >= 2:
                                S.wait(t, "act", b0["act"] + st - 1)
                            S.inc(t.matmul(ps_s[st % 2][:],
                                           kbuf[:, st * 128:(st + 1) * 128],
                                           qb[:, tc * 512:(tc + 1) * 512],
                                           start=True, stop=True), "pe")
                        for tt in range(4):
                            ti = 4 * tc + tt
                            S.wait(t, "dve", b0["dve"] + 10)
                            if tt >= 2:
                                S.wait(t, "dve", b0["dve"] + 10 + 2 * (tt - 1))
                            for st in range(ti + 1):
                                S.inc(t.matmul(ps_o[tt % 2][:],
                                               P_sb[:, st * 512 + tt * 128: st * 512 + (tt + 1) * 128],
                                               vext[:, st * 132:(st + 1) * 132],
                                               start=(st == 0), stop=(st == ti)), "pe",
                                      1 if st == ti else 0)

                    @blk.scalar
                    def _(sc, tc=tc, b0=b0, nst=nst):
                        for st in range(nst):
                            S.wait(sc, "pe", b0["pe"] + st + 1)
                            S.after(sc, "act")
                            S.inc(sc.activation(P_sb[:, st * 512:(st + 1) * 512], ps_s[st % 2][:],
                                                AF.Exp, bias=negc[:], scale=0.0883883476483184), "act")

                    @blk.vector
                    def _(v, tc=tc, b0=b0, nst=nst):
                        for st in range(4 * tc, nst):
                            S.wait(v, "act", b0["act"] + st + 1)
                            for tt in range(4):
                                ti = 4 * tc + tt
                                if ti < st:
                                    S.after(v, "dve")
                                    S.inc(v.memset(P_sb[:, st * 512 + tt * 128: st * 512 + (tt + 1) * 128], 0.0), "dve")
                                elif ti == st:
                                    S.after(v, "dve")
                                    S.inc(v.tensor_mul(P_sb[:, st * 512 + tt * 128: st * 512 + (tt + 1) * 128],
                                                       P_sb[:, st * 512 + tt * 128: st * 512 + (tt + 1) * 128],
                                                       triu_b[:]), "dve")
                        for tt in range(4):
                            S.wait(v, "pe", b0["pe"] + nst + (tt + 1))
                            S.after(v, "dve")
                            S.inc(v.reciprocal(rcol[:], ps_o[tt % 2][:, 128:129]), "dve")
                            S.after(v, "dve")
                            S.wait(v, "st", b0["st"] + 32 * (tt // 2) if tt >= 2 else b0["st"])
                            S.inc(v.tensor_scalar_mul(yout[tt % 2][:], ps_o[tt % 2][:, 0:128], rcol[:]), "dve")

                    @blk.sync
                    def _(sy, tc=tc, h=h, b0=b0):
                        for tt in range(4):
                            ti = 4 * tc + tt
                            sy.wait_ge(S.h["dve"], b0["dve"] + 10 + 2 * (tt + 1))
                            S.inc(sy.dma_start(y1p_h[ti * 128:(ti + 1) * 128, h * 128:(h + 1) * 128],
                                               yout[tt % 2][:]), "st", 16)

    with nc.Block() as blk:
        @blk.sync
        def _(sy):
            sy.wait_ge(S.h["st"], S.c["st"])
            sy.wait_ge(S.h["ld"], S.c["ld"])

    es.close()
    nc.finalize()
    return nc


_NC_CACHE = {}


def _get_nc(key, builder, *args):
    if key not in _NC_CACHE:
        _NC_CACHE[key] = builder(*args)
    return _NC_CACHE[key]


MASKN = ((1 - TRIU) * -1e9).astype(np.float32)
ROT64 = np.concatenate([np.arange(64, 128), np.arange(0, 64)])


def _spmd(nc, in_maps):
    return run_bass_kernel_spmd(nc, in_maps, core_ids=list(range(NCORES))).results


def kernel(L0_attn_hidden_states, L0_attn_Wq, L0_attn_Wk, L0_attn_Wv, L0_attn_Wo,
           L0_attn_Wg, L0_attn_Win, L0_attn_dt_bias, L0_attn_A_log_bias,
           L0_attn_gnorm_w, L0_ffn_W_base, L0_ffn_b_base, L0_ffn_W_router,
           L0_ffn_lora_A, L0_ffn_lora_B, L1_attn_wq, L1_attn_wk, L1_attn_wv,
           L1_attn_wo, L1_ffn_w12, L1_ffn_b12, L1_ffn_w3, L1_ffn_b3):
    C = np.ascontiguousarray
    x0 = C(L0_attn_hidden_states.reshape(16384, HID)).astype(np.float32)
    cos0, sin0 = _rope0_tables()
    cos1, sin1 = _rope1_tables()

    nc1 = _get_nc(("pre", 0), build_pre, 0)
    wq_t = C(L0_attn_Wq.T)
    wk_t = C(L0_attn_Wk.T)
    wv_t = C(L0_attn_Wv.T)
    wg_t = C(L0_attn_Wg.T)
    win_t = C(L0_attn_Win.T)
    in1 = []
    for c in range(NCORES):
        s0 = (c % 2) * TPC
        in1.append({"x": x0[c * TPC:(c + 1) * TPC], "wq": wq_t, "wk": wk_t,
                    "wv": wv_t, "wg": wg_t, "win": win_t,
                    "dtb": L0_attn_dt_bias.reshape(1, 32),
                    "alog": L0_attn_A_log_bias.reshape(1, 32),
                    "rc": C(cos0[s0:s0 + TPC]), "rs": C(sin0[s0:s0 + TPC]),
                    "rsr": C(sin0[s0:s0 + TPC][:, ROT64]),
                    "idf": IDENT})
    r1 = _spmd(nc1, in1)

    qT = np.stack([np.concatenate([r1[2 * b]["qT"], r1[2 * b + 1]["qT"]], axis=2)
                   for b in range(B)])
    kT = np.stack([np.concatenate([r1[2 * b]["kT"], r1[2 * b + 1]["kT"]], axis=2)
                   for b in range(B)])
    ktm = np.stack([np.concatenate([r1[2 * b]["ktm"], r1[2 * b + 1]["ktm"]], axis=0)
                    for b in range(B)])
    xc = np.stack([np.concatenate([r1[2 * b]["xc"], r1[2 * b + 1]["xc"]], axis=0)
                   for b in range(B)])
    gC = np.stack([np.concatenate([r1[2 * b]["gC"], r1[2 * b + 1]["gC"]], axis=1)
                   for b in range(B)])

    nc2 = _get_nc(("mamba",), build_mamba)
    in2 = []
    for c in range(NCORES):
        b = c // 2
        h0 = (c % 2) * HPC
        kv0 = (c % 2) * 4
        in2.append({"qTm": C(qT[b, h0:h0 + HPC]),
                    "kTm": C(kT[b, kv0:kv0 + 4]),
                    "ktm2": C(ktm[b][:, kv0 * 128:(kv0 + 4) * 128]),
                    "xc2": C(xc[b][:, h0 * 128:(h0 + HPC) * 128]),
                    "gC2": C(gC[b, h0:h0 + HPC].reshape(HPC, 32, 128)),
                    "triu": TRIU, "maskn": MASKN, "idf": IDENT})
    r2 = _spmd(nc2, in2)
    del qT, kT, ktm, xc, gC

    y0 = np.empty((16384, HID), dtype=BFNP)
    for c in range(NCORES):
        b = c // 2
        h0 = (c % 2) * HPC
        y0.reshape(B, L, HID)[b, :, h0 * 128:(h0 + HPC) * 128] = r2[c]["y0p"]

    nc3 = _get_nc(("post0",), build_post0)
    wo_t = C(L0_attn_Wo.T)
    wb_t = C(L0_ffn_W_base.T)
    lr = C(np.concatenate([L0_ffn_lora_A.reshape(128, HID), L0_ffn_W_router], axis=0).T)
    bf = C(L0_ffn_lora_B.transpose(0, 2, 1).reshape(128, HID))
    in3 = []
    for c in range(NCORES):
        in3.append({"x": x0[c * TPC:(c + 1) * TPC],
                    "xnfm": r1[c]["xnfm"],
                    "gate": r1[c]["gate"],
                    "y0": C(y0[c * TPC:(c + 1) * TPC]),
                    "wo": wo_t, "wb": wb_t,
                    "bb": L0_ffn_b_base.reshape(1, HID),
                    "lr": lr, "bf": bf,
                    "gn": L0_attn_gnorm_w.reshape(1, 128),
                    "idf": IDENT})
    r3 = _spmd(nc3, in3)
    x1 = np.concatenate([r3[c]["x1"] for c in range(NCORES)], axis=0)
    del y0, r2

    nc4 = _get_nc(("pre", 1), build_pre, 1)
    in4 = []
    for c in range(NCORES):
        s0 = (c % 2) * TPC
        in4.append({"x": x1[c * TPC:(c + 1) * TPC],
                    "wq": C(L1_attn_wq), "wk": C(L1_attn_wk), "wv": C(L1_attn_wv),
                    "rc": C(cos1[s0:s0 + TPC]), "rs": C(sin1[s0:s0 + TPC]),
                    "rsr": C(sin1[s0:s0 + TPC][:, ROT64]),
                    "idf": IDENT})
    r4 = _spmd(nc4, in4)

    qT = np.stack([np.concatenate([r4[2 * b]["qT"], r4[2 * b + 1]["qT"]], axis=2)
                   for b in range(B)])
    kT = np.stack([np.concatenate([r4[2 * b]["kT"], r4[2 * b + 1]["kT"]], axis=2)
                   for b in range(B)])
    vtm = np.stack([np.concatenate([r4[2 * b]["vtm"], r4[2 * b + 1]["vtm"]], axis=0)
                    for b in range(B)])

    nc5 = _get_nc(("attn",), build_attn)
    in5 = []
    for c in range(NCORES):
        b = c // 2
        h0 = (c % 2) * HPC
        kv0 = (c % 2) * 4
        in5.append({"q1": C(qT[b, h0:h0 + HPC]),
                    "k1": C(kT[b, kv0:kv0 + 4]),
                    "v1": C(vtm[b][:, kv0 * 128:(kv0 + 4) * 128]),
                    "triu": TRIU})
    r5 = _spmd(nc5, in5)
    del qT, kT, vtm

    y1 = np.empty((16384, HID), dtype=BFNP)
    for c in range(NCORES):
        b = c // 2
        h0 = (c % 2) * HPC
        y1.reshape(B, L, HID)[b, :, h0 * 128:(h0 + HPC) * 128] = r5[c]["y1p"]

    nc6 = _get_nc(("post1",), build_post1)
    w12_t = C(L1_ffn_w12.T)
    w3_t = C(L1_ffn_w3.T)
    in6 = []
    for c in range(NCORES):
        in6.append({"x1": x1[c * TPC:(c + 1) * TPC],
                    "xnfm": r4[c]["xnfm"],
                    "y1": C(y1[c * TPC:(c + 1) * TPC]),
                    "w12": w12_t, "b12": L1_ffn_b12.reshape(1, 2 * HID),
                    "w3": w3_t, "b3": L1_ffn_b3.reshape(1, HID),
                    "wo": C(L1_attn_wo), "idf": IDENT})
    r6 = _spmd(nc6, in6)
    x2 = np.concatenate([r6[c]["x2"] for c in range(NCORES)], axis=0)
    return np.ascontiguousarray(x2.reshape(B, L, HID)).astype(np.float32)
